# version 02
# baseline (speedup 1.0000x reference)
import contextlib
import numpy as np
import ml_dtypes
import concourse.bass as bass
import concourse.mybir as mybir
from concourse.bass_utils import run_bass_kernel_spmd

"""Tiny raw-Bass emission framework: per-engine step queues + counted semaphores."""

F32 = mybir.dt.float32
BF16 = mybir.dt.bfloat16
U32 = mybir.dt.uint32
AF = mybir.ActivationFunctionType
ALU = mybir.AluOpType
AX = mybir.AxisListType

ENGINES = ("tensor", "vector", "scalar", "gpsimd", "sync")


class Sem:
    __slots__ = ("h", "n", "name")

    def __init__(self, h, name):
        self.h = h
        self.n = 0
        self.name = name

    def bump(self, k):
        self.n += k
        return self.n


class Builder:
    def __init__(self, name="k", paranoid=False):
        self.nc = bass.Bass("TRN2", target_bir_lowering=False, name=name)
        self.q = {e: [] for e in ENGINES}
        self._es = contextlib.ExitStack()
        self._nsem = 0
        self.paranoid = paranoid
        self._allsems = []

    def sem(self, name=None):
        name = name or f"s{self._nsem}"
        self._nsem += 1
        h = self._es.enter_context(self.nc.semaphore(name))
        s = Sem(h, name)
        self._allsems.append(s)
        return s

    def sbuf(self, name, shape, dtype, scope=None):
        es = scope if scope is not None else self._es
        return es.enter_context(self.nc.sbuf_tensor(name, shape, dtype))

    def scope(self):
        import contextlib as _cl
        return _cl.ExitStack()

    def psum(self, name, shape, dtype=F32, scope=None):
        es = scope if scope is not None else self._es
        return es.enter_context(self.nc.psum_tensor(name, shape, dtype))

    def din(self, name, shape, dtype=F32):
        return self.nc.dram_tensor(name, shape, dtype, kind="ExternalInput").ap()

    def dout(self, name, shape, dtype=F32):
        return self.nc.dram_tensor(name, shape, dtype, kind="ExternalOutput").ap()

    def dtmp(self, name, shape, dtype=F32):
        return self.nc.dram_tensor(name, shape, dtype, kind="Internal").ap()

    def step(self, eng, fn, waits=(), inc=None, inc_k=None):
        if self.paranoid:
            cur = [(s, s.n) for s in self._allsems if s.n > 0]
            seen = {id(s) for (s, v) in cur}
            waits = cur + [(s, v) for (s, v) in waits if id(s) not in seen and v is not None]
        val = None
        if inc is not None:
            k = inc_k if inc_k is not None else 1
            val = inc.bump(k)
        for (s, v) in waits:
            if v is None:
                import traceback
                traceback.print_stack()
                raise ValueError(f"None wait value on sem {s.name} for engine {eng}")
        wl = [(s.h, v) for (s, v) in waits]
        semh = inc.h if inc is not None else None
        kk = inc_k if inc_k is not None else 1

        def emit(E, wl=wl, fn=fn, semh=semh, kk=kk):
            for h, v in wl:
                E.wait_ge(h, v)
            ins = fn(E)
            if semh is not None:
                if ins is None:
                    ins = E.nop()
                ins.then_inc(semh, kk)

        self.q[eng].append(emit)
        return val

    def dma(self, eng, out, in_, waits=(), inc=None, **kw):
        return self.step(eng, lambda E: E.dma_start(out=out, in_=in_, **kw),
                         waits=waits, inc=inc, inc_k=16)

    def wait(self, eng, waits):
        self.step(eng, lambda E: None, waits=waits)

    def build(self):
        with self._es:
            with self.nc.Block() as blk:
                for eng in ENGINES:
                    steps = self.q[eng]
                    if not steps:
                        continue
                    getattr(blk, eng)(lambda E, steps=steps: [f(E) for f in steps])
        return self.nc


def shard_rows(x):
    out = []
    for c in range(8):
        out.append(np.ascontiguousarray(
            np.concatenate([x[0, 1024 * c:1024 * (c + 1)],
                            x[1, 1024 * c:1024 * (c + 1)]], axis=0)))
    return out


def unshard_rows(parts, D=1024):
    x = np.empty((2, 8192, D), parts[0].dtype)
    for c in range(8):
        x[0, 1024 * c:1024 * (c + 1)] = parts[c][:1024]
        x[1, 1024 * c:1024 * (c + 1)] = parts[c][1024:]
    return x


class TP:

    def __init__(self, b, pX, idt, T, V):
        self.b, self.pX, self.idt, self.T, self.V = b, pX, idt, T, V
        self.bank = 0
        self.last = {0: None, 1: None}

    def block8(self, src_fn, out_view, waits=()):
        b, T, V = self.b, self.T, self.V
        bi = self.bank
        self.bank ^= 1
        w = list(waits)
        single = self.pX[0] is self.pX[1]
        if single:
            prev = self.last[0] if self.last[0] is not None else self.last[1]
            if self.last[1] is not None and (prev is None or self.last[1] > prev):
                prev = self.last[1]
            if prev is not None:
                w.append((V, prev))
        elif self.last[bi] is not None:
            w.append((V, self.last[bi]))
        td = None
        for j in range(8):
            td = b.step("tensor",
                        lambda E, bi=bi, j=j: E.transpose(self.pX[bi][:, j * 128:(j + 1) * 128],
                                                          src_fn(j), self.idt[:]),
                        waits=(w if j == 0 else ()), inc=(T if j == 7 else None))
        S = getattr(self, "S", None)
        if S is not None:
            ev = b.step("scalar",
                        lambda E, bi=bi: E.copy(out_view,
                                                self.pX[bi][:].rearrange("p (j t) -> p j t", j=8)),
                        waits=[(T, td)], inc=S)
            self.last_S = ev
            self.last[bi] = None
            return ("S", ev)
        ev = b.step("vector",
                    lambda E, bi=bi: E.tensor_copy(out_view,
                                                   self.pX[bi][:].rearrange("p (j t) -> p j t", j=8)),
                    waits=[(T, td)], inc=V)
        self.last[bi] = ev
        return ev


"""Launch B: L0 ffn (dense top-1 MoE) + residual + LayerNorm.  x1_slice -> x2_slice."""

NT = 16


def emit_ln_store(b, acc, i, yt, bi, sq, stats, consts, dst, sems, extra_waits, st_prev):
    L, T, V, S, ST = sems
    s_q = b.step("scalar", lambda E, i=i: E.activation(out=sq[:], in_=acc[:, i, :], func=AF.Square,
                                                       accum_out=stats[:, 1:2]),
                 waits=extra_waits, inc=S)
    v_sx = b.step("vector", lambda E, i=i: E.reduce_sum(stats[:, 0:1], acc[:, i, :], axis=AX.X),
                  waits=extra_waits, inc=V)
    v_mean = b.step("vector", lambda E: E.tensor_scalar_mul(stats[:, 2:3], stats[:, 0:1], 1.0 / 1024.0),
                    waits=[(V, v_sx)], inc=V)
    v_v1 = b.step("vector", lambda E: E.tensor_scalar_mul(stats[:, 3:4], stats[:, 1:2], 1.0 / 1024.0),
                  waits=[(S, s_q)], inc=V)
    v_nm2 = b.step("vector", lambda E: E.tensor_scalar(out=stats[:, 4:5], in0=stats[:, 2:3],
                                                       scalar1=stats[:, 2:3], scalar2=-1.0,
                                                       op0=ALU.mult, op1=ALU.mult),
                   waits=[(V, v_mean)], inc=V)
    v_var = b.step("vector", lambda E: E.tensor_add(stats[:, 3:4], stats[:, 3:4], stats[:, 4:5]),
                   waits=[(V, v_v1), (V, v_nm2)], inc=V)
    s_sd = b.step("scalar", lambda E: E.activation(out=stats[:, 4:5], in_=stats[:, 3:4], func=AF.Sqrt,
                                                   bias=consts[:, 0:1], scale=1.0),
                  waits=[(V, v_var)], inc=S)
    v_rstd = b.step("vector", lambda E: E.reciprocal(stats[:, 5:6], stats[:, 4:5]), waits=[(S, s_sd)], inc=V)
    v_nm = b.step("vector", lambda E: E.tensor_scalar(out=stats[:, 2:3], in0=stats[:, 2:3],
                                                      scalar1=stats[:, 5:6], scalar2=-1.0,
                                                      op0=ALU.mult, op1=ALU.mult),
                  waits=[(V, v_rstd)], inc=V)
    wsto = [(ST, st_prev)] if st_prev is not None else []
    v_y = b.step("vector", lambda E, i=i, bi=bi: E.tensor_scalar(out=yt[:, bi, :], in0=acc[:, i, :],
                                                                 scalar1=stats[:, 5:6], scalar2=stats[:, 2:3],
                                                                 op0=ALU.mult, op1=ALU.add),
                 waits=[(V, v_nm)] + wsto, inc=V)
    return b.dma("sync", dst, yt[:, bi, :], waits=[(V, v_y)], inc=ST)


def emit_cast_transpose(b, src_tile_fn, n_tiles, xbf, XT, idt, pX, sems, src_waits):
    L, T, V, S, ST = sems
    tr_done, v_evac = {}, {}
    for i in range(n_tiles):
        bi = i % 2
        waits = list(src_waits(i))
        if i >= 2:
            waits.append((V, v_evac[i - 2]))
        cast_done = b.step("vector", lambda E, bi=bi, i=i: E.tensor_copy(xbf[:, bi, :], src_tile_fn(i)),
                           waits=waits, inc=V)
        tw = [(V, cast_done)]
        if i >= 2:
            tw.append((V, v_evac[i - 2]))
        for j in range(8):
            tr_done[i] = b.step("tensor",
                                lambda E, bi=bi, j=j: E.transpose(pX[bi][:, j * 128:(j + 1) * 128],
                                                                 xbf[:, bi, j * 128:(j + 1) * 128], idt[:]),
                                waits=(tw if j == 0 else ()), inc=(T if j == 7 else None))
        v_evac[i] = b.step("vector",
                           lambda E, bi=bi, i=i: E.tensor_copy(
                               XT[:, :, i * 128:(i + 1) * 128],
                               pX[bi][:].rearrange("p (j t) -> p j t", j=8)),
                           waits=[(T, tr_done[i])], inc=V)
    return v_evac


def build_B():
    b = Builder("launchB", paranoid=True)
    x_in = b.din("x_in", [2048, 1024], F32)
    w_red_in = b.din("w_red_in", [1024, 16], F32)
    wg_in = b.din("wg_in", [8, 16], F32)
    W1_in = b.din("W1_in", [8, 1024, 2048], F32)
    b1_in = b.din("b1_in", [8, 2048], F32)
    W2_in = b.din("W2_in", [8, 2048, 1024], F32)
    b2_in = b.din("b2_in", [8, 1024], F32)
    idt_in = b.din("idt_in", [128, 128], BF16)
    ones_in = b.din("ones_in", [1, 512], BF16)
    consts_in = b.din("consts_in", [128, 4], F32)
    x_out = b.dout("x_out", [2048, 1024], F32)

    xt_in = x_in.rearrange("(i p) d -> i p d", p=128)
    yt_out = x_out.rearrange("(i p) d -> i p d", p=128)

    acc = b.sbuf("acc", [128, NT, 1024], F32)
    xbf = b.sbuf("xbf", [128, 2, 1024], BF16)
    XT = b.sbuf("XT", [128, 8, 2048], BF16)
    idt = b.sbuf("idt", [128, 128], BF16)
    ones = b.sbuf("ones", [1, 512], BF16)
    consts = b.sbuf("consts", [128, 4], F32)
    w_redb = b.sbuf("w_redb", [128, 8, 16], BF16)
    wg_sb = b.sbuf("wg_sb", [8, 16], F32)
    wgsq = b.sbuf("wgsq", [8, 16], F32)
    rn8 = b.sbuf("rn8", [8, 1], F32)
    wgn_bf = b.sbuf("wgn_bf", [8, 16], BF16)
    wgTn = b.sbuf("wgTn", [16, 8], BF16)
    xrT = b.sbuf("xrT", [16, 4, 512], BF16)
    lg = b.sbuf("lg", [128, 8], F32)
    m1 = b.sbuf("m1", [128, 1], F32)
    negm = b.sbuf("negm", [128, 1], F32)
    probs = b.sbuf("probs", [128, 8], F32)
    sums = b.sbuf("sums", [128, 1], F32)
    score = b.sbuf("score", [128, 1], F32)
    me = b.sbuf("me", [128, NT, 8], F32)
    W1b = b.sbuf("W1b", [128, 8, 2048], BF16)
    W2b = b.sbuf("W2b", [128, 16, 1024], BF16)
    b1sb = b.sbuf("b1sb", [128, 16], F32)
    b2row = b.sbuf("b2row", [1, 1024], BF16)
    hT = b.sbuf("hT", [128, 16, 512], BF16)
    sq = b.sbuf("sq", [128, 1024], F32)
    stats = b.sbuf("stats", [128, 6], F32)
    yt = b.sbuf("yt", [128, 2, 1024], F32)

    pX = [b.psum("pX0", [128, 1024], BF16), b.psum("pX1", [128, 1024], BF16)]
    pR = b.psum("pR", [128, 512], F32)
    pH = [b.psum("pH0", [128, 512], F32), b.psum("pH1", [128, 512], F32)]
    pO = [b.psum("pO0", [128, 512], F32), b.psum("pO1", [128, 512], F32)]
    pWT = b.psum("pWT", [128, 1024], BF16)

    L = b.sem("L"); G = b.sem("G"); T = b.sem("T"); V = b.sem("V"); S = b.sem("S"); ST = b.sem("ST")
    sems = (L, T, V, S, ST)

    lv = {}
    for i in range(NT):
        lv[i] = b.dma("sync", acc[:, i, :], xt_in[i], inc=L)
    l_idt = b.dma("sync", idt[:], idt_in, inc=L)
    l_ones = b.dma("sync", ones[:], ones_in, inc=L)
    l_consts = b.dma("sync", consts[:], consts_in, inc=L)
    l_wg = b.dma("sync", wg_sb[:], wg_in, inc=L)
    g_wred = b.dma("gpsimd", w_redb[:], w_red_in.rearrange("(c p) g -> p c g", p=128), inc=G)

    v_evac = emit_cast_transpose(b, lambda i: acc[:, i, :], NT, xbf, XT, idt, pX, sems,
                                 src_waits=lambda i: [(L, lv[i]), (L, l_idt)])

    s_sq = b.step("scalar", lambda E: E.activation(out=wgsq[:], in_=wg_sb[:], func=AF.Square),
                  waits=[(L, l_wg)], inc=S)
    v_rs = b.step("vector", lambda E: E.reduce_sum(rn8[:], wgsq[:], axis=AX.X), waits=[(S, s_sq)], inc=V)
    s_sqrt = b.step("scalar", lambda E: E.sqrt(rn8[:], rn8[:]), waits=[(V, v_rs)], inc=S)
    v_mx = b.step("vector", lambda E: E.tensor_scalar_max(rn8[:], rn8[:], 1e-4), waits=[(S, s_sqrt)], inc=V)
    v_rc = b.step("vector", lambda E: E.reciprocal(rn8[:], rn8[:]), waits=[(V, v_mx)], inc=V)
    v_wgn = b.step("vector", lambda E: E.tensor_scalar(out=wgn_bf[:], in0=wg_sb[:], scalar1=rn8[:],
                                                       scalar2=None, op0=ALU.mult),
                   waits=[(V, v_rc)], inc=V)
    t_wgT = b.step("tensor", lambda E: E.transpose(pWT[0:16, 0:8], wgn_bf[:], idt[0:8, 0:8]),
                   waits=[(V, v_wgn), (L, l_idt)], inc=T)
    v_wgT = b.step("vector", lambda E: E.tensor_copy(wgTn[:], pWT[0:16, 0:8]),
                   waits=[(T, t_wgT)], inc=V)

    xr_done = {}
    for tg in range(4):
        def mm_xr(E, tg=tg):
            for c in range(8):
                ins = E.matmul(pR[0:16, :], w_redb[:, c, :], XT[:, c, tg * 512:(tg + 1) * 512],
                               start=(c == 0), stop=(c == 7))
            return ins
        w = [(G, g_wred), (V, v_evac[tg * 4 + 3])]
        if tg > 0:
            w.append((V, xr_done[tg - 1]))
        tv = b.step("tensor", mm_xr, waits=w, inc=T)
        xr_done[tg] = b.step("vector", lambda E, tg=tg: E.tensor_copy(xrT[:, tg, :], pR[0:16, :]),
                             waits=[(T, tv)], inc=V)

    prev_vlg = xr_done[3]
    for i in range(NT):
        tg, tt = i // 4, i % 4
        t_lg = b.step("tensor",
                      lambda E, tg=tg, tt=tt: E.matmul(pR[:, 0:8], xrT[:, tg, tt * 128:(tt + 1) * 128],
                                                       wgTn[:], start=True, stop=True),
                      waits=[(V, prev_vlg), (V, v_wgT)], inc=T)
        v_lg = b.step("vector", lambda E: E.tensor_copy(lg[:], pR[:, 0:8]), waits=[(T, t_lg)], inc=V)
        prev_vlg = v_lg
        v_m1 = b.step("vector", lambda E: E.reduce_max(m1[:], lg[:], axis=AX.X), waits=[(V, v_lg)], inc=V)
        v_ng = b.step("vector", lambda E: E.tensor_scalar_mul(negm[:], m1[:], -1.0), waits=[(V, v_m1)], inc=V)
        s_ex = b.step("scalar", lambda E: E.activation(out=probs[:], in_=lg[:], func=AF.Exp,
                                                       bias=negm[:], scale=1.0, accum_out=sums[:]),
                      waits=[(V, v_ng)], inc=S)
        v_sc = b.step("vector", lambda E: E.reciprocal(score[:], sums[:]), waits=[(S, s_ex)], inc=V)
        v_oh = b.step("vector", lambda E: E.tensor_scalar(out=probs[:], in0=lg[:], scalar1=m1[:],
                                                          scalar2=None, op0=ALU.is_ge),
                      waits=[(V, v_sc)], inc=V)
        b.step("vector", lambda E, i=i: E.tensor_scalar(out=me[:, i, :], in0=probs[:], scalar1=score[:],
                                                        scalar2=None, op0=ALU.mult),
               waits=[(V, v_oh)], inc=V)

    t_last_W_use = T.n
    prev_stt = {0: None, 1: None}
    last_gelu = None
    for e in range(8):
        wts = []
        if e > 0:
            wts = [(S, last_gelu), (V, prev_stt[0]), (V, prev_stt[1])]
        g_w1 = b.dma("gpsimd", W1b[:], W1_in[e].rearrange("(c p) h -> p c h", p=128), waits=wts, inc=G)
        g_w2 = b.dma("gpsimd", W2b[:], W2_in[e].rearrange("(c p) d -> p c d", p=128), inc=G)
        g_b2 = b.dma("gpsimd", b2row[:], b2_in[e].unsqueeze(0), inc=G)
        l_b1 = b.dma("sync", b1sb[:], b1_in[e].rearrange("(c p) -> p c", p=128),
                     inc=L, allow_slow_non_contiguous=True)
        s_gelu = {}
        for tg in range(4):
            for hc in range(16):
                bk = hc % 2
                def mm_h(E, tg=tg, hc=hc, bk=bk):
                    for c in range(8):
                        ins = E.matmul(pH[bk][:], W1b[:, c, hc * 128:(hc + 1) * 128],
                                       XT[:, c, tg * 512:(tg + 1) * 512],
                                       start=(c == 0), stop=(c == 7))
                    return ins
                w = [(G, g_w1)]
                if (tg, hc) not in ((0, 0), (0, 1)):
                    prev = (tg, hc - 2) if hc >= 2 else (tg - 1, 14 + hc)
                    w.append((S, s_gelu[prev]))
                tmm = b.step("tensor", mm_h, waits=w, inc=T)
                s_gelu[(tg, hc)] = b.step("scalar",
                                          lambda E, hc=hc, bk=bk: E.activation(
                                              out=hT[:, hc, :], in_=pH[bk][:], func=AF.Gelu,
                                              bias=b1sb[:, hc:hc + 1], scale=1.0),
                                          waits=[(T, tmm), (L, l_b1)], inc=S)
            for tt in range(4):
                i = tg * 4 + tt
                for cc in range(2):
                    def mm_o(E, tt=tt, cc=cc):
                        E.matmul(pO[cc][:], ones[0:1, 0:128], b2row[:, cc * 512:(cc + 1) * 512],
                                 start=True, stop=False)
                        for hc in range(16):
                            ins = E.matmul(pO[cc][:], hT[:, hc, tt * 128:(tt + 1) * 128],
                                           W2b[:, hc, cc * 512:(cc + 1) * 512],
                                           start=False, stop=(hc == 15))
                        return ins
                    w = [(S, s_gelu[(tg, 15)]), (G, g_w2), (G, g_b2), (L, l_ones)]
                    if prev_stt[cc] is not None:
                        w.append((V, prev_stt[cc]))
                    tmo = b.step("tensor", mm_o, waits=w, inc=T)
                    prev_stt[cc] = b.step("vector",
                                          lambda E, i=i, cc=cc, e=e: E.scalar_tensor_tensor(
                                              out=acc[:, i, cc * 512:(cc + 1) * 512],
                                              in0=pO[cc][:], scalar=me[:, i, e:e + 1],
                                              in1=acc[:, i, cc * 512:(cc + 1) * 512],
                                              op0=ALU.mult, op1=ALU.add),
                                          waits=[(T, tmo)], inc=V)
        last_gelu = s_gelu[(3, 15)]
        t_last_W_use = T.n
    v_exp_done = V.n

    st_prev = {}
    for i in range(NT):
        sp = st_prev.get(i - 2)
        st_prev[i] = emit_ln_store(b, acc, i, yt, i % 2, sq, stats, consts, yt_out[i], sems,
                                   extra_waits=[(V, v_exp_done), (L, l_consts)], st_prev=sp)
    b.wait("sync", [(ST, ST.n)])
    return b.build()


def host_inputs_B(I, x1_slice):
    return dict(
        x_in=np.ascontiguousarray(x1_slice),
        w_red_in=I["L0_ffn_w_red"], wg_in=I["L0_ffn_wg"],
        W1_in=I["L0_ffn_W1"], b1_in=I["L0_ffn_b1"],
        W2_in=I["L0_ffn_W2"], b2_in=I["L0_ffn_b2"],
        idt_in=np.eye(128, dtype=ml_dtypes.bfloat16),
        ones_in=np.ones((1, 512), ml_dtypes.bfloat16),
        consts_in=np.tile(np.array([[1e-5, 0, 0, 0]], np.float32), (128, 1)),
    )




"""Launch C: L1 attn (delta-rule fast-weight, scan over the 2-batch axis)
+ residual + LayerNorm.  x2_slice -> x3_slice.
Row layout per core: tiles 0..7 = batch0 (t=0) seq chunk, tiles 8..15 = batch1 (t=1)."""

NT = 16


def build_C():
    b = Builder("launchC", paranoid=True)
    x_in = b.din("x_in", [2048, 1024], F32)
    Wk_in = b.din("Wk_in", [1024, 1024], F32)
    Wv_in = b.din("Wv_in", [1024, 1024], F32)
    Wq_in = b.din("Wq_in", [1024, 1024], F32)
    idt_in = b.din("idt_in", [128, 128], BF16)
    consts_in = b.din("consts_in", [128, 4], F32)
    x_out = b.dout("x_out", [2048, 1024], F32)

    xt_in = x_in.rearrange("(i p) d -> i p d", p=128)
    yt_out = x_out.rearrange("(i p) d -> i p d", p=128)

    xf32 = b.sbuf("xf32", [128, 1, 1024], F32)
    xbf = b.sbuf("xbf", [128, 2, 1024], BF16)
    XT = b.sbuf("XT", [128, 8, 2048], BF16)
    idt = b.sbuf("idt", [128, 128], BF16)
    consts = b.sbuf("consts", [128, 4], F32)
    wraw = b.sbuf("wraw", [128, 2, 1024], BF16)
    WT = [b.sbuf(f"WT{w}", [128, 8, 1024], BF16) for w in range(3)]
    kvq = b.sbuf("kvq", [128, 6, 1024], F32)
    Sst = b.sbuf("Sst", [128, 8192], F32)
    tmp = b.sbuf("tmp", [128, 8192], F32)
    rec = b.sbuf("rec", [128, 3, 1024], F32)
    bsb0 = b.sbuf("bsb0", [128, 128], F32)
    stats = b.sbuf("stats", [128, 6], F32)
    yt = b.sbuf("yt", [128, 1, 1024], F32)
    x2pair = b.sbuf("x2pair", [128, 2, 1024], F32)

    pX = [b.psum("pX0", [128, 1024], BF16), b.psum("pX1", [128, 1024], BF16)]
    pA = [b.psum(f"pA{k}", [128, 512], F32) for k in range(4)]

    L = b.sem("L"); G = b.sem("G"); T = b.sem("T"); V = b.sem("V"); S = b.sem("S"); ST = b.sem("ST")
    sems = (L, T, V, S, ST)
    tp = TP(b, pX, idt, T, V)

    l_idt = b.dma("sync", idt[:], idt_in, inc=L)
    l_consts = b.dma("sync", consts[:], consts_in, inc=L)

    cast_of, lxv, ev_xt = {}, {}, {}
    for i in range(NT):
        bi = i % 2
        w = [] if i == 0 else [(V, cast_of[i - 1])]
        lxv[i] = b.dma("sync", xf32[:, 0, :], xt_in[i], waits=w, inc=L)
        prev_tr = [(V, ev_xt[i - 2])] if i >= 2 else []
        cast_of[i] = b.step("vector", lambda E, bi=bi: E.tensor_copy(xbf[:, bi, :], xf32[:, 0, :]),
                            waits=[(L, lxv[i])] + prev_tr, inc=V)
        ev_xt[i] = tp.block8(lambda j, bi=bi: xbf[:, bi, j * 128:(j + 1) * 128],
                             XT[:, :, i * 128:(i + 1) * 128],
                             waits=[(V, cast_of[i]), (L, l_idt)])
    v_xt_done = V.n

    W_ins = [Wk_in, Wv_in, Wq_in]
    wrow_ev = {}
    for w in range(3):
        for r in range(8):
            bi = r % 2
            gw = [] if (w, r) < (0, 2) else [(V, wrow_ev[(w, r - 2) if r >= 2 else (w - 1, 6 + r)])]
            g_w = b.dma("gpsimd", wraw[:, bi, :],
                        W_ins[w][r * 128:(r + 1) * 128, :], waits=gw, inc=G)
            wrow_ev[(w, r)] = tp.block8(
                lambda c, bi=bi: wraw[:, bi, c * 128:(c + 1) * 128],
                WT[w][:, :, r * 128:(r + 1) * 128],
                waits=[(G, g_w)])
    v_wt_done = V.n

    S4 = Sst[:].rearrange("p (n i j) -> p n i j", n=128, i=8)

    def v4i(x):
        return x.rearrange("p (n j) -> p n j", n=128).unsqueeze(2).broadcast_to([128, 128, 8, 8])

    def v4j(x):
        return x.rearrange("p (n i) -> p n i", n=128).unsqueeze(3).broadcast_to([128, 128, 8, 8])

    T4 = tmp[:].rearrange("p (n i j) -> p n i j", n=128, i=8)

    st_prev = {}
    prev_pair_done = None
    for jp in range(8):
        tiles = (jp, jp + 8)
        slot_ev = {}
        for wsel in range(3):
            for ti in range(2):
                slot = wsel * 2 + ti
                for nh in range(2):
                    bk = (2 * slot + nh) % 4
                    def mm1(E, wsel=wsel, bk=bk, nh=nh, tile=tiles[ti]):
                        for c in range(8):
                            ins = E.matmul(pA[bk][:], XT[:, c, tile * 128:(tile + 1) * 128],
                                           WT[wsel][:, c, nh * 512:(nh + 1) * 512],
                                           start=(c == 0), stop=(c == 7))
                        return ins
                    w = [(V, v_wt_done), (V, v_xt_done)]
                    if prev_pair_done is not None:
                        w.append((V, prev_pair_done))
                    prev = slot_ev.get((slot - 2, nh))
                    if prev is not None:
                        w.append((V, prev))
                    tm = b.step("tensor", mm1, waits=w, inc=T)
                    slot_ev[(slot, nh)] = b.step("vector",
                                                 lambda E, slot=slot, nh=nh, bk=bk: E.tensor_copy(
                                                     kvq[:, slot, nh * 512:(nh + 1) * 512], pA[bk][:]),
                                                 waits=[(T, tm)], inc=V)
        v_kvq = V.n
        lp = b.dma("sync", x2pair[:, 0, :], xt_in[tiles[0]],
                   waits=[(ST, st_prev[jp - 1])] if jp >= 1 else [], inc=L)
        lp2 = b.dma("sync", x2pair[:, 1, :], xt_in[tiles[1]], inc=L)

        k0, k1, v0, v1, q0, q1 = (kvq[:, s, :] for s in range(6))
        bs = bsb0[:]
        bsb = bs.rearrange("p (n o) -> p n o", n=128).broadcast_to([128, 128, 8])
        Sq = rec[:, 0, :]

        def knorm(kx, wv):
            s1 = b.step("scalar", lambda E: E.activation(out=tmp[:, 0:1024], in_=kx, func=AF.Square),
                        waits=[(V, wv)], inc=S)
            r1 = b.step("vector", lambda E: E.reduce_sum(
                bs.rearrange("p (n o) -> p n o", n=128),
                tmp[:, 0:1024].rearrange("p (n i) -> p n i", n=128), axis=AX.X),
                waits=[(S, s1)], inc=V)
            s2 = b.step("scalar", lambda E: E.sqrt(bs, bs), waits=[(V, r1)], inc=S)
            r2 = b.step("vector", lambda E: E.tensor_scalar_add(bs, bs, 1e-6), waits=[(S, s2)], inc=V)
            r3 = b.step("vector", lambda E: E.reciprocal(bs, bs), waits=[(V, r2)], inc=V)
            kx3 = kx.rearrange("p (n i) -> p n i", n=128)
            r4 = b.step("vector", lambda E: E.tensor_tensor(kx3, kx3, bsb, ALU.mult),
                        waits=[(V, r3)], inc=V)
            return r4

        vk0 = knorm(k0, v_kvq)
        v_s1 = b.step("vector", lambda E: E.tensor_tensor(S4, v4j(v0), v4i(k0), ALU.mult),
                      waits=[(V, vk0)], inc=V)
        s_th = b.step("scalar", lambda E: E.activation(out=Sst[:], in_=Sst[:], func=AF.Tanh),
                      waits=[(V, v_s1)], inc=S)
        v_sq = b.step("vector", lambda E: E.tensor_tensor(T4, S4, v4i(q0), ALU.mult),
                      waits=[(S, s_th)], inc=V)
        v_r = b.step("vector", lambda E: E.reduce_sum(Sq.unsqueeze(2), T4.rearrange("p n i j -> p (n i) j"), axis=AX.X),
                     waits=[(V, v_sq)], inc=V)
        s_si = b.step("scalar", lambda E: E.activation(out=rec[:, 1, :], in_=Sq, func=AF.Silu),
                      waits=[(V, v_r)], inc=S)
        v_o = b.step("vector", lambda E: E.tensor_mul(rec[:, 2, :], Sq, rec[:, 1, :]),
                     waits=[(S, s_si)], inc=V)
        v_ln0 = b.step("vector", lambda E: E.tensor_add(rec[:, 2, :], rec[:, 2, :], x2pair[:, 0, :]),
                       waits=[(V, v_o), (L, lp)], inc=V)
        st_a = emit_ln_store(b, rec, 2, yt, 0, tmp[:, 0:1024], stats, consts, yt_out[tiles[0]], sems,
                             extra_waits=[(V, v_ln0), (L, l_consts)], st_prev=st_prev.get(jp - 1))
        vk1 = knorm(k1, v_ln0)
        v_rt = b.step("vector", lambda E: E.tensor_tensor(T4, S4, v4i(k1), ALU.mult),
                      waits=[(V, vk1)], inc=V)
        v_r2 = b.step("vector", lambda E: E.reduce_sum(Sq.unsqueeze(2), T4.rearrange("p n i j -> p (n i) j"), axis=AX.X),
                      waits=[(V, v_rt)], inc=V)
        v_d = b.step("vector", lambda E: E.tensor_sub(v1, v1, Sq), waits=[(V, v_r2)], inc=V)
        v_op = b.step("vector", lambda E: E.tensor_tensor(T4, v4j(v1), v4i(k1), ALU.mult),
                      waits=[(V, v_d)], inc=V)
        v_s2 = b.step("vector", lambda E: E.tensor_add(Sst[:], Sst[:], tmp[:]), waits=[(V, v_op)], inc=V)
        s_th2 = b.step("scalar", lambda E: E.activation(out=Sst[:], in_=Sst[:], func=AF.Tanh),
                       waits=[(V, v_s2)], inc=S)
        v_sq2 = b.step("vector", lambda E: E.tensor_tensor(T4, S4, v4i(q1), ALU.mult),
                       waits=[(S, s_th2)], inc=V)
        v_r3 = b.step("vector", lambda E: E.reduce_sum(Sq.unsqueeze(2), T4.rearrange("p n i j -> p (n i) j"), axis=AX.X),
                      waits=[(V, v_sq2)], inc=V)
        s_si2 = b.step("scalar", lambda E: E.activation(out=rec[:, 1, :], in_=Sq, func=AF.Silu),
                       waits=[(V, v_r3)], inc=S)
        v_o2 = b.step("vector", lambda E: E.tensor_mul(rec[:, 2, :], Sq, rec[:, 1, :]),
                      waits=[(S, s_si2)], inc=V)
        v_ln1 = b.step("vector", lambda E: E.tensor_add(rec[:, 2, :], rec[:, 2, :], x2pair[:, 1, :]),
                       waits=[(V, v_o2), (L, lp2)], inc=V)
        st_prev[jp] = emit_ln_store(b, rec, 2, yt, 0, tmp[:, 0:1024], stats, consts, yt_out[tiles[1]], sems,
                                    extra_waits=[(V, v_ln1), (L, l_consts)], st_prev=st_a)
        prev_pair_done = V.n
    b.wait("sync", [(ST, ST.n)])
    return b.build()


def host_inputs_C(I, x_slice):
    return dict(
        x_in=np.ascontiguousarray(x_slice),
        Wk_in=I["L1_attn_W_k"], Wv_in=I["L1_attn_W_v"], Wq_in=I["L1_attn_W_q"],
        idt_in=np.eye(128, dtype=ml_dtypes.bfloat16),
        consts_in=np.tile(np.array([[1e-5, 0, 0, 0]], np.float32), (128, 1)),
    )




"""Launch A: L0 MLA causal attention + residual + LayerNorm.  x0 -> x1_slice.
v1: latents computed replicated from x_full; dense-padded causal via mask-matmul;
2-pass exact softmax with PSUM-resident scores."""

SCALE_A = float(128 ** -0.5)
NT = 16


def build_A():
    b = Builder("launchA", paranoid=True)
    x_full = b.din("x_full", [2, 8192, 1024], F32)
    x_own = b.din("x_own", [2048, 1024], F32)
    wq_in = b.din("wq_in", [1024, 1024], F32)
    wkd_in = b.din("wkd_in", [1024, 64], F32)
    wvd_in = b.din("wvd_in", [1024, 64], F32)
    wku_in = b.din("wku_in", [64, 128], F32)
    wvu_in = b.din("wvu_in", [64, 128], F32)
    wo_in = b.din("wo_in", [1024, 1024], F32)
    cm_in = b.din("cm_in", [8, 16, 128, 512], BF16)
    idt_in = b.din("idt_in", [128, 128], BF16)
    idtf_in = b.din("idtf_in", [128, 128], F32)
    onesf_in = b.din("onesf_in", [1, 128], F32)
    consts_in = b.din("consts_in", [128, 4], F32)
    x_out = b.dout("x_out", [2048, 1024], F32)

    xo_t = x_own.rearrange("(i p) d -> i p d", p=128)
    yt_out = x_out.rearrange("(i p) d -> i p d", p=128)

    klatT = b.sbuf("klatT", [64, 2, 8192], BF16)
    vlatT = b.sbuf("vlatT", [64, 2, 8192], BF16)
    QT = b.sbuf("QT", [128, 8, 1024], BF16)
    KT = b.sbuf("KT", [128, 8192], BF16)
    Vtm = b.sbuf("Vtm", [128, 64, 128], BF16)
    cmask = b.sbuf("cmask", [128, 16, 512], BF16)
    YT = b.sbuf("YT", [128, 8, 1024], BF16)
    P = b.sbuf("P", [128, 2, 512], BF16)
    PT = b.sbuf("PT", [128, 64, 128], BF16)
    rb = b.sbuf("rb", [128, 128], F32)
    rrow = b.sbuf("rrow", [1, 128], F32)
    xbf = b.sbuf("xbf", [128, 2, 1024], BF16)
    XTt = b.sbuf("XTt", [128, 2, 8, 128], BF16)
    wbig = b.sbuf("wbig", [128, 8, 1024], BF16)
    wkdb = b.sbuf("wkdb", [128, 8, 64], BF16)
    wvdb = b.sbuf("wvdb", [128, 8, 64], BF16)
    wkub = b.sbuf("wkub", [64, 128], BF16)
    wvub = b.sbuf("wvub", [64, 128], BF16)
    idt = b.sbuf("idt", [128, 128], BF16)
    idtf = b.sbuf("idtf", [128, 128], F32)
    onesf = b.sbuf("onesf", [1, 128], F32)
    consts = b.sbuf("consts", [128, 4], F32)
    cmx = b.sbuf("cmx", [128, 16], F32)
    sums_c = b.sbuf("sums_c", [128, 16], F32)
    mxs = b.sbuf("mxs", [128, 4], F32)
    xown = b.sbuf("xown", [128, 1024], F32)
    lnin = b.sbuf("lnin", [128, 1, 1024], F32)
    stats = b.sbuf("stats", [128, 6], F32)
    yt = b.sbuf("yt", [128, 1, 1024], F32)

    pX = [b.psum("pX0", [128, 1024], BF16), b.psum("pX1", [128, 1024], BF16)]
    pS = [b.psum("pS0", [128, 512], F32), b.psum("pS1", [128, 512], F32)]
    pT_ = b.psum("pT", [128, 1024], BF16)
    pO = b.psum("pO", [128, 512], F32)
    pW = [b.psum("pW0", [128, 512], F32), b.psum("pW1", [128, 512], F32)]

    L = b.sem("L"); G = b.sem("G"); T = b.sem("T"); V = b.sem("V"); S = b.sem("S"); ST = b.sem("ST")
    sems = (L, T, V, S, ST)
    tp = TP(b, pX, idt, T, V)

    l_idt = b.dma("sync", idt[:], idt_in, inc=L)
    l_idtf = b.dma("sync", idtf[:], idtf_in, inc=L)
    l_onesf = b.dma("sync", onesf[:], onesf_in, inc=L)
    l_consts = b.dma("sync", consts[:], consts_in, inc=L)
    g_wq = b.dma("gpsimd", wbig[:], wq_in.rearrange("(c p) d -> p c d", p=128), inc=G)
    g_wkd = b.dma("gpsimd", wkdb[:], wkd_in.rearrange("(c p) d -> p c d", p=128), inc=G)
    g_wvd = b.dma("gpsimd", wvdb[:], wvd_in.rearrange("(c p) d -> p c d", p=128), inc=G)
    g_wku = b.dma("gpsimd", wkub[:], wku_in, inc=G)
    g_wvu = b.dma("gpsimd", wvub[:], wvu_in, inc=G)

    prev_latev = None
    xload = {}
    for i in range(128):
        bb_, col = i // 64, (i % 64) * 128
        w = [(V, tp.last[i % 2])] if i >= 2 and tp.last[i % 2] is not None else []
        xload[i] = b.dma("gpsimd", xbf[:, i % 2, :],
                         x_full[bb_, col:col + 128, :], waits=w, inc=G)
        ev = tp.block8(lambda j, bi=i % 2: xbf[:, bi, j * 128:(j + 1) * 128],
                       XTt[:, i % 2, :, :], waits=[(G, xload[i]), (L, l_idt)])
        w0 = [(V, ev), (G, g_wkd), (G, g_wvd)]
        if prev_latev is not None:
            w0.append((V, prev_latev))
        tk = b.step("tensor", lambda E, bi=i % 2: [E.matmul(pS[0][0:64, 0:128], wkdb[:, c, :],
                                                            XTt[:, bi, c, :], start=(c == 0), stop=(c == 7))
                                                   for c in range(8)][-1], waits=w0, inc=T)
        tv = b.step("tensor", lambda E, bi=i % 2: [E.matmul(pS[1][0:64, 0:128], wvdb[:, c, :],
                                                            XTt[:, bi, c, :], start=(c == 0), stop=(c == 7))
                                                   for c in range(8)][-1], inc=T)
        e1 = b.step("vector", lambda E, bb_=bb_, col=col: E.tensor_copy(
            klatT[:, bb_, col:col + 128], pS[0][0:64, 0:128]), waits=[(T, tk)], inc=V)
        prev_latev = b.step("vector", lambda E, bb_=bb_, col=col: E.tensor_copy(
            vlatT[:, bb_, col:col + 128], pS[1][0:64, 0:128]), waits=[(T, tv)], inc=V)

    g_wo = None
    prev_qev = prev_latev
    t_prev_kv_use = T.n
    v_ps_guard = V.n
    t_prev_mask_use = T.n
    v_prev_yt = None
    s_prev_exp = None
    v_prev_wo = {0: None, 1: None}
    st_prev = None
    v_prev_lat_use = None
    for bb_ in range(2):
        if bb_ == 1:
            g_wq = b.dma("gpsimd", wbig[:], wq_in.rearrange("(c p) d -> p c d", p=128),
                         waits=[(V, v_prev_wo[1])], inc=G)
        for i in range(bb_ * 8, bb_ * 8 + 8):
            col = (i % 8) * 128
            w = [(V, tp.last[i % 2])] if tp.last[i % 2] is not None else []
            gx = b.dma("gpsimd", xbf[:, i % 2, :], xo_t[i], waits=w, inc=G)
            ev = tp.block8(lambda j, bi=i % 2: xbf[:, bi, j * 128:(j + 1) * 128],
                           XTt[:, i % 2, :, :], waits=[(G, gx)])
            for h in range(8):
                bk = h % 2
                def mm_q(E, bi=i % 2, h=h, bk=bk):
                    for c in range(8):
                        ins = E.matmul(pS[bk][:, 0:128], wbig[:, c, h * 128:(h + 1) * 128],
                                       XTt[:, bi, c, :], start=(c == 0), stop=(c == 7))
                    return ins
                w1 = [(V, ev), (G, g_wq)]
                if prev_qev is not None:
                    w1.append((V, prev_qev))
                tq = b.step("tensor", mm_q, waits=w1, inc=T)
                prev_qev = b.step("vector", lambda E, h=h, bb_=bb_, col=col, bk=bk: E.tensor_copy(
                    QT[:, h, col:col + 128], pS[bk][:, 0:128]),
                    waits=[(T, tq)], inc=V)

        g_wo = b.dma("gpsimd", wbig[:], wo_in.rearrange("(c p) d -> p c d", p=128),
                     waits=[(V, prev_qev)], inc=G)
        v_ps_guard = prev_qev
        for kv in range(2):
            exp_ev = []
            for kc in range(16):
                def mm_kt(E, kc=kc, bb_=bb_, kv=kv):
                    return E.matmul(pS[kc % 2][:], wkub[kv * 32:(kv + 1) * 32, :], klatT[kv * 32:(kv + 1) * 32, bb_, kc * 512:(kc + 1) * 512],
                                    start=True, stop=True)
                w2 = [(G, g_wku)]
                if kc < 2:
                    w2.append((T, t_prev_kv_use))
                    w2.append((V, v_ps_guard))
                    if s_prev_exp is not None:
                        w2.append((S, s_prev_exp))
                    if v_prev_yt is not None:
                        w2.append((V, v_prev_yt))
                else:
                    w2.append((V, exp_ev[kc - 2]))
                tm = b.step("tensor", mm_kt, waits=w2, inc=T)
                exp_ev.append(b.step("vector", lambda E, kc=kc: E.tensor_copy(
                    KT[:, kc * 512:(kc + 1) * 512], pS[kc % 2][:]), waits=[(T, tm)], inc=V))
            vexp = {}
            for g4 in range(16):
                bk = g4 % 2
                def mm_v(E, g4=g4, bb_=bb_, kv=kv, bk=bk):
                    for j in range(4):
                        kt = g4 * 4 + j
                        ins = E.matmul(pS[bk][:, j * 128:(j + 1) * 128],
                                       vlatT[kv * 32:(kv + 1) * 32, bb_, kt * 128:(kt + 1) * 128],
                                       wvub[kv * 32:(kv + 1) * 32, :], start=True, stop=True)
                    return ins
                w3 = [(G, g_wvu), (V, exp_ev[14 + (g4 % 2)] if g4 < 2 else vexp[g4 - 2])]
                if g4 < 2 and v_prev_yt is not None:
                    w3.append((V, v_prev_yt))
                tm = b.step("tensor", mm_v, waits=w3, inc=T)
                vexp[g4] = b.step("vector", lambda E, g4=g4, bk=bk: E.tensor_copy(
                    Vtm[:, g4 * 4:(g4 + 1) * 4, :].rearrange("p a d -> p (a d)"), pS[bk][:]),
                    waits=[(T, tm)], inc=V)
            v_kv_ready = V.n
            for tt in range(8):
                lmw = [(T, t_prev_mask_use)]
                if s_prev_exp is not None:
                    lmw.append((S, s_prev_exp))
                lm = b.dma("sync", cmask[:], cm_in[tt].rearrange("k p j -> p k j"),
                           waits=lmw, inc=L)
                for rep in range(4):
                    h = kv * 4 + rep
                    qsl = QT[:, h, tt * 128:(tt + 1) * 128]
                    red = {}
                    for kc in range(16):
                        bk = kc % 2
                        def mm_s(E, kc=kc, bk=bk, qsl=qsl):
                            E.matmul(pS[bk][:], qsl, KT[:, kc * 512:(kc + 1) * 512], start=True, stop=False)
                            return E.matmul(pS[bk][:], idt[:], cmask[:, kc, :], start=False, stop=True)
                        w4 = [(L, lm), (V, v_kv_ready)]
                        if kc >= 2:
                            w4.append((V, red[kc - 2]))
                        elif s_prev_exp is not None:
                            w4.append((S, s_prev_exp))
                        tm = b.step("tensor", mm_s, waits=w4, inc=T)
                        red[kc] = b.step("vector", lambda E, kc=kc, bk=bk: E.reduce_max(
                            cmx[:, kc:kc + 1], pS[bk][:], axis=AX.X), waits=[(T, tm)], inc=V)
                    v_mx = b.step("vector", lambda E: E.reduce_max(mxs[:, 0:1], cmx[:], axis=AX.X),
                                  waits=[(V, red[15])], inc=V)
                    v_ng = b.step("vector", lambda E: E.tensor_scalar_mul(mxs[:, 1:2], mxs[:, 0:1], -SCALE_A),
                                  waits=[(V, v_mx)], inc=V)
                    s_exp, tr_of, ptev, pv_of = {}, {}, {}, {}
                    for kc in range(16):
                        bk = kc % 2
                        def mm_s2(E, kc=kc, bk=bk, qsl=qsl):
                            E.matmul(pS[bk][:], qsl, KT[:, kc * 512:(kc + 1) * 512], start=True, stop=False)
                            return E.matmul(pS[bk][:], idt[:], cmask[:, kc, :], start=False, stop=True)
                        w5 = [(V, v_ng)]
                        if kc >= 2:
                            w5.append((S, s_exp[kc - 2]))
                        tm = b.step("tensor", mm_s2, waits=w5, inc=T)
                        we = [(T, tm)]
                        if kc >= 2:
                            we.append((T, tr_of[kc - 2]))
                        s_exp[kc] = b.step("scalar", lambda E, kc=kc, bk=bk: E.activation(
                            out=P[:, kc % 2, :], in_=pS[bk][:], func=AF.Exp,
                            bias=mxs[:, 1:2], scale=SCALE_A, accum_out=sums_c[:, kc:kc + 1]),
                            waits=we, inc=S)
                        wtr = [(S, s_exp[kc])]
                        if kc >= 1:
                            wtr.append((V, ptev[kc - 1]))
                        for j in range(4):
                            tr_of[kc] = b.step("tensor", lambda E, kc=kc, j=j: E.transpose(
                                pT_[:, j * 128:(j + 1) * 128],
                                P[:, kc % 2, j * 128:(j + 1) * 128], idt[:]),
                                waits=(wtr if j == 0 else ()), inc=(T if j == 3 else None))
                        wpt = [(T, tr_of[kc])]
                        if kc == 0 and v_prev_yt is not None:
                            wpt.append((V, v_prev_yt))
                        ptev[kc] = b.step("vector", lambda E, kc=kc: E.tensor_copy(
                            PT[:, kc * 4:(kc + 1) * 4, :].rearrange("p a d -> p (a d)"),
                            pT_[:, 0:512]), waits=wpt, inc=V)
                    s_prev_exp = s_exp[15]
                    def mm_pv(E):
                        for kt in range(64):
                            ins = E.matmul(pO[:, 0:128], Vtm[:, kt, :], PT[:, kt, :],
                                           start=(kt == 0), stop=(kt == 63))
                        return ins
                    w6 = [(V, ptev[15])]
                    if v_prev_yt is not None:
                        w6.append((V, v_prev_yt))
                    pv_all = b.step("tensor", mm_pv, waits=w6, inc=T)
                    pv_of[15] = pv_all
                    v_sm = b.step("vector", lambda E: E.reduce_sum(mxs[:, 2:3], sums_c[:], axis=AX.X),
                                  waits=[(S, s_exp[15])], inc=V)
                    v_rc = b.step("vector", lambda E: E.reciprocal(mxs[:, 3:4], mxs[:, 2:3]),
                                  waits=[(V, v_sm)], inc=V)
                    t_tr = b.step("tensor", lambda E: E.transpose(pO[0:1, 256:384], mxs[:, 3:4], idtf[:]),
                                  waits=[(V, v_rc), (T, pv_of[15]), (L, l_idtf)], inc=T)
                    v_rr = b.step("vector", lambda E: E.tensor_copy(rrow[:], pO[0:1, 256:384]),
                                  waits=[(T, t_tr)], inc=V)
                    t_bc = b.step("tensor", lambda E: E.matmul(pO[:, 128:256], onesf[:], rrow[:],
                                                               start=True, stop=True),
                                  waits=[(L, l_onesf), (V, v_rr)], inc=T)
                    v_rb = b.step("vector", lambda E: E.tensor_copy(rb[:], pO[:, 128:256]),
                                  waits=[(T, t_bc)], inc=V)
                    v_prev_yt = b.step("vector", lambda E, h=h, tt=tt: E.tensor_tensor(
                        YT[:, h, tt * 128:(tt + 1) * 128], pO[:, 0:128], rb[:], ALU.mult),
                        waits=[(V, v_rb)], inc=V)
                t_prev_mask_use = T.n
            t_prev_kv_use = T.n
        for tt in range(8):
            i = bb_ * 8 + tt
            lxo = b.dma("sync", xown[:], xo_t[i],
                        waits=([(ST, st_prev)] if st_prev is not None else []), inc=L)
            for cc in range(2):
                def mm_wo(E, tt=tt, cc=cc):
                    for h in range(8):
                        ins = E.matmul(pW[cc][:], YT[:, h, tt * 128:(tt + 1) * 128],
                                       wbig[:, h, cc * 512:(cc + 1) * 512],
                                       start=(h == 0), stop=(h == 7))
                    return ins
                w7 = [(G, g_wo), (V, v_prev_yt)]
                if v_prev_wo[cc] is not None:
                    w7.append((V, v_prev_wo[cc]))
                tw = b.step("tensor", mm_wo, waits=w7, inc=T)
                v_prev_wo[cc] = b.step("vector", lambda E, cc=cc: E.scalar_tensor_tensor(
                    out=lnin[:, 0, cc * 512:(cc + 1) * 512], in0=pW[cc][:], scalar=1.0,
                    in1=xown[:, cc * 512:(cc + 1) * 512], op0=ALU.mult, op1=ALU.add),
                    waits=[(T, tw), (L, lxo)], inc=V)
            st_prev = emit_ln_store(b, lnin, 0, yt, 0, xown, stats, consts, yt_out[i], sems,
                                    extra_waits=[(V, v_prev_wo[0]), (V, v_prev_wo[1]), (L, l_consts)],
                                    st_prev=st_prev)
    b.wait("sync", [(ST, ST.n)])
    return b.build()


def host_inputs(I, core, x_full):
    cm = np.empty((8, 16, 128, 512), np.float32)
    q0s = 1024 * core + 128 * np.arange(8)
    for tt in range(8):
        qpos = q0s[tt] + np.arange(128)[:, None]
        for kc in range(16):
            kpos = kc * 512 + np.arange(512)[None, :]
            cm[tt, kc] = np.where(kpos <= qpos, 0.0, -3.0e38)
    xo = np.concatenate([x_full[0, 1024 * core:1024 * (core + 1)],
                         x_full[1, 1024 * core:1024 * (core + 1)]], axis=0)
    return dict(
        x_full=x_full, x_own=np.ascontiguousarray(xo),
        wq_in=I["L0_attn_wq"], wkd_in=I["L0_attn_wk_down"], wvd_in=I["L0_attn_wv_down"],
        wku_in=np.concatenate([I["L0_attn_wk_up"]]*2, 0), wvu_in=np.concatenate([I["L0_attn_wv_up"]]*2, 0), wo_in=I["L0_attn_wo"],
        cm_in=cm.astype(ml_dtypes.bfloat16),
        idt_in=np.eye(128, dtype=ml_dtypes.bfloat16),
        idtf_in=np.eye(128, dtype=np.float32),
        onesf_in=np.ones((1, 128), np.float32),
        consts_in=np.tile(np.array([[1e-5, 0, 0, 0]], np.float32), (128, 1)),
    )




"""Launch D (two parts): L1 ffn block (pre-norm MHA + top-2 MoE) + residual + final LN.
D-pre:  x3_slice -> own kT/v slices (bf16).
host:   concatenate -> kT_all [1024, 16384], v_all [16384, 1024].
D-main: x3_slice + kT_all + v_all + weights -> x4_slice."""

SCALE_D = 0.125
NT = 16
RSQ = 1.0 / 1024.0


def emit_rms_norm_tile(b, sems, src_ap, scr, stats, consts_eps_col, xnbf_dst, waits):
    L, T, V, S, ST = sems
    s1 = b.step("scalar", lambda E: E.activation(out=scr[:], in_=src_ap, func=AF.Square,
                                                 accum_out=stats[:, 0:1]), waits=waits, inc=S)
    v1 = b.step("vector", lambda E: E.tensor_scalar_mul(stats[:, 1:2], stats[:, 0:1], RSQ),
                waits=[(S, s1)], inc=V)
    s2 = b.step("scalar", lambda E: E.activation(out=stats[:, 2:3], in_=stats[:, 1:2], func=AF.Sqrt,
                                                 bias=consts_eps_col, scale=1.0),
                waits=[(V, v1)], inc=S)
    v2 = b.step("vector", lambda E: E.reciprocal(stats[:, 3:4], stats[:, 2:3]), waits=[(S, s2)], inc=V)
    v3 = b.step("vector", lambda E: E.tensor_scalar(out=xnbf_dst, in0=src_ap, scalar1=stats[:, 3:4],
                                                    scalar2=None, op0=ALU.mult),
                waits=[(V, v2)], inc=V)
    return v3


def build_Dpre():
    b = Builder("launchDpre", paranoid=True)
    x_in = b.din("x_in", [2048, 1024], F32)
    wk_in = b.din("wk_in", [1024, 1024], F32)
    wv_in = b.din("wv_in", [1024, 1024], F32)
    n1_in = b.din("n1_in", [1024], F32)
    idt_in = b.din("idt_in", [128, 128], BF16)
    consts_in = b.din("consts_in", [128, 4], F32)
    kT_out = b.dout("kT_out", [1024, 2048], BF16)
    v_out = b.dout("v_out", [2048, 1024], BF16)

    xt_in = x_in.rearrange("(i p) d -> i p d", p=128)
    kT_o = kT_out.rearrange("(c p) t -> p c t", p=128)
    v_o = v_out.rearrange("(i p) d -> i p d", p=128)

    xf32 = b.sbuf("xf32", [128, 2, 1024], F32)
    xbf = b.sbuf("xbf", [128, 2, 1024], BF16)
    XTt = b.sbuf("XTt", [128, 2, 8, 128], BF16)
    scr = b.sbuf("scr", [128, 1024], F32)
    stats = b.sbuf("stats", [128, 6], F32)
    idt = b.sbuf("idt", [128, 128], BF16)
    consts = b.sbuf("consts", [128, 4], F32)
    w1sb = b.sbuf("w1sb", [128, 8], F32)
    wkb = b.sbuf("wkb", [128, 8, 1024], BF16)
    wvb = b.sbuf("wvb", [128, 8, 1024], BF16)
    ktile = b.sbuf("ktile", [128, 8, 128], BF16)
    vtile = b.sbuf("vtile", [128, 1024], BF16)

    pX = [b.psum("pX0", [128, 1024], BF16), b.psum("pX1", [128, 1024], BF16)]
    pK = [b.psum("pK0", [128, 512], F32), b.psum("pK1", [128, 512], F32)]
    pV = [b.psum("pV0", [128, 512], F32), b.psum("pV1", [128, 512], F32)]

    L = b.sem("L"); G = b.sem("G"); T = b.sem("T"); V = b.sem("V"); S = b.sem("S"); ST = b.sem("ST")
    sems = (L, T, V, S, ST)
    tp = TP(b, pX, idt, T, V)

    l_idt = b.dma("sync", idt[:], idt_in, inc=L)
    l_consts = b.dma("sync", consts[:], consts_in, inc=L)
    l_w1 = b.dma("sync", w1sb[:], n1_in.rearrange("(c p) -> p c", p=128), inc=L,
                 allow_slow_non_contiguous=True)
    g_wk = b.dma("gpsimd", wkb[:], wk_in.rearrange("(c p) d -> p c d", p=128), inc=G)
    g_wv = b.dma("gpsimd", wvb[:], wv_in.rearrange("(c p) d -> p c d", p=128), inc=G)

    st_k, st_v = {}, {}
    prev_kev = None
    prev_vev = None
    for i in range(NT):
        bi = i % 2
        wl = [] if i < 2 else [(V, norm_of[i - 2])]
        lx = b.dma("sync", xf32[:, bi, :], xt_in[i], waits=wl, inc=L)
        if i == 0:
            norm_of = {}
        norm_of[i] = emit_rms_norm_tile(b, sems, xf32[:, bi, :], scr, stats, consts[:, 1:2],
                                        xbf[:, bi, :],
                                        waits=[(L, lx), (L, l_consts)] +
                                              ([(V, ev_of[i - 2])] if i >= 2 else []))
        if i == 0:
            ev_of = {}
        ev_of[i] = tp.block8(lambda j, bi=bi: xbf[:, bi, j * 128:(j + 1) * 128],
                             XTt[:, bi, :, :], waits=[(V, norm_of[i]), (L, l_idt)])
        hv = b.step("vector", lambda E, bi=bi: E.tensor_tensor(
            XTt[:, bi, :, :], XTt[:, bi, :, :],
            w1sb[:].unsqueeze(2).broadcast_to([128, 8, 128]), ALU.mult),
            waits=[(V, ev_of[i]), (L, l_w1)], inc=V)
        kev = {}
        for f in range(8):
            bk = f % 2
            def mm_k(E, bi=bi, f=f, bk=bk):
                for c in range(8):
                    ins = E.matmul(pK[bk][:, 0:128], wkb[:, c, f * 128:(f + 1) * 128],
                                   XTt[:, bi, c, :], start=(c == 0), stop=(c == 7))
                return ins
            w = [(V, hv), (G, g_wk)]
            if f >= 2:
                w.append((V, kev[f - 2]))
            elif prev_kev is not None:
                w.append((V, prev_kev))
            tm = b.step("tensor", mm_k, waits=w, inc=T)
            kw = [(T, tm)]
            if f == 0 and i >= 1:
                kw.append((ST, st_k[i - 1]))
            kev[f] = b.step("vector", lambda E, f=f, bk=bk: E.tensor_copy(
                ktile[:, f, :], pK[bk][:, 0:128]), waits=kw, inc=V)
        prev_kev = kev[7]
        st_k[i] = b.dma("sync", kT_o[:, :, i * 128:(i + 1) * 128], ktile[:],
                        waits=[(V, kev[7])], inc=ST)
        vev = {}
        for cc in range(2):
            def mm_v(E, bi=bi, cc=cc):
                for c in range(8):
                    ins = E.matmul(pV[cc][:], XTt[:, bi, c, :], wvb[:, c, cc * 512:(cc + 1) * 512],
                                   start=(c == 0), stop=(c == 7))
                return ins
            w = [(V, hv), (G, g_wv)]
            if prev_vev is not None:
                w.append((V, prev_vev))
            tm = b.step("tensor", mm_v, waits=w, inc=T)
            vw = [(T, tm)]
            if cc == 0 and i >= 1:
                vw.append((ST, st_v[i - 1]))
            vev[cc] = b.step("vector", lambda E, cc=cc: E.tensor_copy(
                vtile[:, cc * 512:(cc + 1) * 512], pV[cc][:]), waits=vw, inc=V)
        prev_vev = vev[1]
        st_v[i] = b.dma("sync", v_o[i], vtile[:], waits=[(V, vev[1])], inc=ST)
    b.wait("sync", [(ST, ST.n)])
    return b.build()


def build_Dmain():
    b = Builder("launchDmain", paranoid=True)
    x_in = b.din("x_in", [2048, 1024], F32)
    kT_all = b.din("kT_all", [1024, 16384], BF16)
    v_all = b.din("v_all", [16384, 1024], BF16)
    wq_in = b.din("wq_in", [1024, 1024], F32)
    wo_in = b.din("wo_in", [1024, 1024], F32)
    n1_in = b.din("n1_in", [1024], F32)
    n2_in = b.din("n2_in", [1024], F32)
    rw_in = b.din("rw_in", [1024, 8], F32)
    shg_in = b.din("shg_in", [1024, 2048], F32)
    shu_in = b.din("shu_in", [1024, 2048], F32)
    shd_in = b.din("shd_in", [2048, 1024], F32)
    eg_in = b.din("eg_in", [8, 1024, 2048], F32)
    eu_in = b.din("eu_in", [8, 1024, 2048], F32)
    ed_in = b.din("ed_in", [8, 2048, 1024], F32)
    idt_in = b.din("idt_in", [128, 128], BF16)
    idtf_in = b.din("idtf_in", [128, 128], F32)
    onesf_in = b.din("onesf_in", [1, 128], F32)
    consts_in = b.din("consts_in", [128, 4], F32)
    x_out = b.dout("x_out", [2048, 1024], F32)

    xt_in = x_in.rearrange("(i p) d -> i p d", p=128)
    yt_out = x_out.rearrange("(i p) d -> i p d", p=128)

    acc = b.sbuf("acc", [128, NT, 1024], F32)
    xbf = b.sbuf("xbf", [128, 2, 1024], BF16)
    XTt = b.sbuf("XTt", [128, 2, 8, 128], BF16)
    scr = b.sbuf("scr", [128, 1024], F32)
    stats = b.sbuf("stats", [128, 8], F32)
    idt = b.sbuf("idt", [128, 128], BF16)
    idtf = b.sbuf("idtf", [128, 128], F32)
    onesf = b.sbuf("onesf", [1, 128], F32)
    consts = b.sbuf("consts", [128, 4], F32)
    w1sb = b.sbuf("w1sb", [128, 8], F32)
    w2sb = b.sbuf("w2sb", [128, 8], F32)
    mxs = b.sbuf("mxs", [128, 4], F32)
    cmx = b.sbuf("cmx", [128, 4], F32)
    sums_c = b.sbuf("sums_c", [128, 4], F32)
    rb = b.sbuf("rb", [128, 128], F32)
    rrow = b.sbuf("rrow", [1, 128], F32)
    me = b.sbuf("me", [128, NT, 8], F32)
    rt = b.sbuf("rt", [128, 8], F32)
    rt2 = b.sbuf("rt2", [128, 8], F32)
    lnin = b.sbuf("lnin", [128, 1, 1024], F32)
    yt = b.sbuf("yt", [128, 1, 1024], F32)

    L = b.sem("L"); G = b.sem("G"); T = b.sem("T"); V = b.sem("V"); S = b.sem("S"); ST = b.sem("ST")
    sems = (L, T, V, S, ST)

    sc1 = b.scope()
    QT = b.sbuf("QT", [128, 8, 2048], BF16, scope=sc1)
    YT = b.sbuf("YT", [128, 8, 1024], BF16, scope=sc1)
    KT2 = b.sbuf("KT2", [128, 8192], BF16, scope=sc1)
    V2 = b.sbuf("V2", [128, 64, 128], BF16, scope=sc1)
    P = b.sbuf("P", [128, 2048], BF16, scope=sc1)
    PT = b.sbuf("PT", [128, 2, 1024], BF16, scope=sc1)
    wqb = b.sbuf("wqb", [128, 8, 1024], BF16, scope=sc1)
    wob = b.sbuf("wob", [128, 8, 1024], BF16, scope=sc1)
    pSb = b.psum("pSb", [128, 2048], F32, scope=sc1)
    pX0 = b.psum("pX0", [128, 1024], BF16, scope=sc1)
    pT_ = b.psum("pT", [128, 1024], BF16, scope=sc1)
    pO = b.psum("pO", [128, 512], F32, scope=sc1)
    pW = [pSb[:, 0:512], pSb[:, 512:1024]]
    tp = TP(b, [pX0, pX0], idt, T, V)

    l_idt = b.dma("sync", idt[:], idt_in, inc=L)
    l_idtf = b.dma("sync", idtf[:], idtf_in, inc=L)
    l_onesf = b.dma("sync", onesf[:], onesf_in, inc=L)
    l_consts = b.dma("sync", consts[:], consts_in, inc=L)
    l_w1 = b.dma("sync", w1sb[:], n1_in.rearrange("(c p) -> p c", p=128), inc=L,
                 allow_slow_non_contiguous=True)
    l_w2 = b.dma("sync", w2sb[:], n2_in.rearrange("(c p) -> p c", p=128), inc=L,
                 allow_slow_non_contiguous=True)
    g_wq = b.dma("gpsimd", wqb[:], wq_in.rearrange("(c p) d -> p c d", p=128), inc=G)

    lx = {}
    for i in range(NT):
        lx[i] = b.dma("sync", acc[:, i, :], xt_in[i], inc=L)
    prev_qev = None
    for i in range(NT):
        bi = i % 2
        nw = [(L, lx[i]), (L, l_consts)]
        if i == 0:
            ev_of = {}
        if i >= 2:
            nw.append((V, ev_of[i - 2]))
        nv = emit_rms_norm_tile(b, sems, acc[:, i, :], scr, stats, consts[:, 1:2],
                                xbf[:, bi, :], waits=nw)
        ev_of[i] = tp.block8(lambda j, bi=bi: xbf[:, bi, j * 128:(j + 1) * 128],
                             XTt[:, bi, :, :], waits=[(V, nv), (L, l_idt)])
        hv = b.step("vector", lambda E, bi=bi: E.tensor_tensor(
            XTt[:, bi, :, :], XTt[:, bi, :, :],
            w1sb[:].unsqueeze(2).broadcast_to([128, 8, 128]), ALU.mult),
            waits=[(V, ev_of[i]), (L, l_w1)], inc=V)
        for hg in range(2):
            def mm_q(E, bi=bi, hg=hg):
                for hp in range(4):
                    for c in range(8):
                        ins = E.matmul(pSb[:, hp * 512 + 0:hp * 512 + 128],
                                       wqb[:, c, (hg * 4 + hp) * 128:(hg * 4 + hp + 1) * 128],
                                       XTt[:, bi, c, :], start=(c == 0), stop=(c == 7))
                return ins
            w = [(V, hv), (G, g_wq)]
            if prev_qev is not None:
                w.append((V, prev_qev))
            tq = b.step("tensor", mm_q, waits=w, inc=T)
            prev_qev = b.step("vector", lambda E, i=i, hg=hg: E.tensor_copy(
                QT[:, hg * 4:(hg + 1) * 4, i * 128:(i + 1) * 128],
                pSb[:].rearrange("p (hp o) -> p hp o", hp=4)[:, :, 0:128]),
                waits=[(T, tq)], inc=V)

    g_wo = b.dma("gpsimd", wob[:], wo_in.rearrange("(c p) d -> p c d", p=128), inc=G)
    v_prev_wo = {0: None, 1: None}
    v_prev_yt = None
    s_prev_exp = None
    t_prev_kv = T.n
    v_ps_guard = prev_qev
    for bb_ in range(2):
        for hp in range(8):
            kvw = [(T, t_prev_kv)]
            if s_prev_exp is not None:
                kvw.append((S, s_prev_exp))
            if v_prev_yt is not None:
                kvw.append((V, v_prev_yt))
            lkt = b.dma("sync", KT2[:], kT_all[hp * 128:(hp + 1) * 128, bb_ * 8192:(bb_ + 1) * 8192],
                        waits=kvw, inc=L)
            lv2 = b.dma("sync", V2[:],
                        v_all[bb_ * 8192:(bb_ + 1) * 8192, hp * 128:(hp + 1) * 128]
                        .rearrange("(kt p) d -> p kt d", p=128), inc=L)
            for tt in range(8):
                for h01 in range(2):
                    base = h01 * 64
                    qsl = QT[base:base + 64, hp, tt * 128:(tt + 1) * 128]
                    red = {}
                    for g in range(4):
                        def mm_s(E, g=g, qsl=qsl, base=base):
                            for j in range(4):
                                kc = g * 4 + j
                                ins = E.matmul(pSb[:, j * 512:(j + 1) * 512], qsl,
                                               KT2[base:base + 64, kc * 512:(kc + 1) * 512],
                                               start=True, stop=True)
                            return ins
                        w = [(L, lkt), (L, lv2)]
                        if g > 0:
                            w.append((V, red[g - 1]))
                        else:
                            w.append((V, v_ps_guard))
                            if s_prev_exp is not None:
                                w.append((S, s_prev_exp))
                        tm = b.step("tensor", mm_s, waits=w, inc=T)
                        red[g] = b.step("vector", lambda E, g=g: E.reduce_max(
                            cmx[:, g:g + 1], pSb[:], axis=AX.X), waits=[(T, tm)], inc=V)
                    v_mx = b.step("vector", lambda E: E.reduce_max(mxs[:, 0:1], cmx[:], axis=AX.X),
                                  waits=[(V, red[3])], inc=V)
                    v_ng = b.step("vector", lambda E: E.tensor_scalar_mul(mxs[:, 1:2], mxs[:, 0:1], -SCALE_D),
                                  waits=[(V, v_mx)], inc=V)
                    v_ps_guard = v_ng
                    s_exp, ptev, pv_of = {}, {}, {}
                    tr_of = {}
                    for g in range(4):
                        def mm_s2(E, g=g, qsl=qsl, base=base):
                            for j in range(4):
                                kc = g * 4 + j
                                ins = E.matmul(pSb[:, j * 512:(j + 1) * 512], qsl,
                                               KT2[base:base + 64, kc * 512:(kc + 1) * 512],
                                               start=True, stop=True)
                            return ins
                        w = [(V, v_ng)]
                        if g > 0:
                            w.append((S, s_exp[g - 1]))
                        tm = b.step("tensor", mm_s2, waits=w, inc=T)
                        we = [(T, tm)]
                        if g > 0:
                            we.append((T, tr_of[(g - 1, 1)]))
                        s_exp[g] = b.step("scalar", lambda E, g=g: E.activation(
                            out=P[:], in_=pSb[:], func=AF.Exp,
                            bias=mxs[:, 1:2], scale=SCALE_D, accum_out=sums_c[:, g:g + 1]),
                            waits=we, inc=S)
                        for half in range(2):
                            wtr = [(S, s_exp[g])]
                            if (g, half) != (0, 0):
                                prevh = (g, half - 1) if half == 1 else (g - 1, 1)
                                wtr.append((V, ptev[prevh]))
                            for j in range(8):
                                tr_of[(g, half)] = b.step("tensor", lambda E, half=half, j=j: E.transpose(
                                    pT_[:, j * 128:(j + 1) * 128],
                                    P[:, half * 1024 + j * 128:half * 1024 + (j + 1) * 128], idt[:]),
                                    waits=(wtr if j == 0 else ()), inc=(T if j == 7 else None))
                            wpt = [(T, tr_of[(g, half)])]
                            if (g, half) >= (1, 0):
                                prevh = (g - 1, half)
                                wpt.append((T, pv_of[prevh]))
                            ptev[(g, half)] = b.step("vector", lambda E, half=half: E.tensor_copy(
                                PT[:, half, :], pT_[:]), waits=wpt, inc=V)
                            def mm_pv(E, g=g, half=half, base=base):
                                for j in range(8):
                                    kt = g * 16 + half * 8 + j
                                    ins = E.matmul(pO[base:base + 64, 0:128],
                                                   V2[:, kt, base:base + 64],
                                                   PT[:, half, j * 128:(j + 1) * 128],
                                                   start=(kt == 0), stop=(kt == 63))
                                return ins
                            w6 = [(V, ptev[(g, half)])]
                            if g == 0 and half == 0 and v_prev_yt is not None:
                                w6.append((V, v_prev_yt))
                            pv_of[(g, half)] = b.step("tensor", mm_pv, waits=w6, inc=T)
                    s_prev_exp = s_exp[3]
                    v_sm = b.step("vector", lambda E: E.reduce_sum(mxs[:, 2:3], sums_c[:], axis=AX.X),
                                  waits=[(S, s_exp[3])], inc=V)
                    v_rc = b.step("vector", lambda E: E.reciprocal(mxs[:, 3:4], mxs[:, 2:3]),
                                  waits=[(V, v_sm)], inc=V)
                    t_tr = b.step("tensor", lambda E: E.transpose(pO[0:1, 256:384], mxs[:, 3:4], idtf[:]),
                                  waits=[(V, v_rc), (T, pv_of[(3, 1)]), (L, l_idtf)], inc=T)
                    v_rr = b.step("vector", lambda E: E.tensor_copy(rrow[:], pO[0:1, 256:384]),
                                  waits=[(T, t_tr)], inc=V)
                    t_bc = b.step("tensor", lambda E, base=base: E.matmul(
                        pO[base:base + 64, 128:256], onesf[:, 0:64], rrow[:], start=True, stop=True),
                        waits=[(L, l_onesf), (V, v_rr)], inc=T)
                    v_rb = b.step("vector", lambda E, base=base: E.tensor_copy(
                        rb[base:base + 64, :], pO[base:base + 64, 128:256]),
                        waits=[(T, t_bc)], inc=V)
                    v_prev_yt = b.step("vector", lambda E, hp=hp, tt=tt, base=base: E.tensor_tensor(
                        YT[base:base + 64, hp, tt * 128:(tt + 1) * 128],
                        pO[base:base + 64, 0:128], rb[base:base + 64, :], ALU.mult),
                        waits=[(V, v_rb)], inc=V)
            t_prev_kv = T.n
        for tt in range(8):
            i = bb_ * 8 + tt
            for cc in range(2):
                def mm_wo(E, tt=tt, cc=cc):
                    for hpx in range(8):
                        ins = E.matmul(pW[cc], YT[:, hpx, tt * 128:(tt + 1) * 128],
                                       wob[:, hpx, cc * 512:(cc + 1) * 512],
                                       start=(hpx == 0), stop=(hpx == 7))
                    return ins
                w7 = [(G, g_wo), (V, v_prev_yt), (S, s_prev_exp)]
                if v_prev_wo[cc] is not None:
                    w7.append((V, v_prev_wo[cc]))
                tw = b.step("tensor", mm_wo, waits=w7, inc=T)
                v_prev_wo[cc] = b.step("vector", lambda E, i=i, cc=cc: E.scalar_tensor_tensor(
                    out=acc[:, i, cc * 512:(cc + 1) * 512], in0=pW[cc], scalar=1.0,
                    in1=acc[:, i, cc * 512:(cc + 1) * 512], op0=ALU.mult, op1=ALU.add),
                    waits=[(T, tw)], inc=V)
        v_ps_guard = V.n
    v_attn_done = V.n
    sc1.close()

    sc2 = b.scope()
    h2T = b.sbuf("h2T", [128, 8, 2048], BF16, scope=sc2)
    gu = b.sbuf("gu", [128, 2, 2, 8, 128], BF16, scope=sc2)
    dn = b.sbuf("dn", [128, 16, 1024], BF16, scope=sc2)
    aT = b.sbuf("aT", [128, 16, 512], BF16, scope=sc2)
    gsb = b.sbuf("gsb", [128, 512], BF16, scope=sc2)
    rwb = b.sbuf("rwb", [128, 8, 8], BF16, scope=sc2)
    pX1 = b.psum("pX1", [128, 1024], BF16, scope=sc2)
    pG = [b.psum("pG0", [128, 512], F32, scope=sc2), b.psum("pG1", [128, 512], F32, scope=sc2)]
    pU = [b.psum("pU0", [128, 512], F32, scope=sc2), b.psum("pU1", [128, 512], F32, scope=sc2)]
    pD = [b.psum("pD0", [128, 512], F32, scope=sc2), b.psum("pD1", [128, 512], F32, scope=sc2)]
    pR = b.psum("pR", [128, 512], F32, scope=sc2)
    tp2 = TP(b, [pX1, pX1], idt, T, V)

    t_attn_done = T.n
    b.wait("gpsimd", [(T, t_attn_done), (V, v_attn_done), (S, s_prev_exp)])
    g_rw = b.dma("gpsimd", rwb[:], rw_in.rearrange("(c p) e -> p c e", p=128), inc=G)

    prev_lg = None
    for i in range(NT):
        bi = i % 2
        nw = [(V, v_attn_done), (L, l_consts)]
        if i == 0:
            ev2_of = {}
        if i >= 2:
            nw.append((V, ev2_of[i - 2]))
        nv = emit_rms_norm_tile(b, sems, acc[:, i, :], scr, stats, consts[:, 1:2],
                                xbf[:, bi, :], waits=nw)
        ev2_of[i] = tp2.block8(lambda j, bi=bi: xbf[:, bi, j * 128:(j + 1) * 128],
                               XTt[:, bi, :, :], waits=[(V, nv)])
        hv = b.step("vector", lambda E, bi=bi: E.tensor_tensor(
            XTt[:, bi, :, :], XTt[:, bi, :, :],
            w2sb[:].unsqueeze(2).broadcast_to([128, 8, 128]), ALU.mult),
            waits=[(V, ev2_of[i]), (L, l_w2)], inc=V)
        hev = b.step("vector", lambda E, bi=bi, i=i: E.tensor_copy(
            h2T[:, :, i * 128:(i + 1) * 128], XTt[:, bi, :, :]),
            waits=[(V, hv)], inc=V)
        def mm_r(E, i=i):
            for c in range(8):
                ins = E.matmul(pR[:, 0:8], h2T[:, c, i * 128:(i + 1) * 128], rwb[:, c, :],
                               start=(c == 0), stop=(c == 7))
            return ins
        w = [(V, hev), (G, g_rw)]
        if prev_lg is not None:
            w.append((V, prev_lg))
        tr_ = b.step("tensor", mm_r, waits=w, inc=T)
        v_lg = b.step("vector", lambda E: E.tensor_copy(rt[:], pR[:, 0:8]), waits=[(T, tr_)], inc=V)
        prev_lg = v_lg
        v_m = b.step("vector", lambda E: E.reduce_max(mxs[:, 0:1], rt[:], axis=AX.X, negate=True),
                     waits=[(V, v_lg)], inc=V)
        s_e = b.step("scalar", lambda E: E.activation(out=rt[:], in_=rt[:], func=AF.Exp,
                                                      bias=mxs[:, 0:1], scale=1.0,
                                                      accum_out=mxs[:, 1:2]),
                     waits=[(V, v_m)], inc=S)
        v_rc = b.step("vector", lambda E: E.reciprocal(mxs[:, 2:3], mxs[:, 1:2]), waits=[(S, s_e)], inc=V)
        v_p = b.step("vector", lambda E: E.tensor_scalar(out=rt[:], in0=rt[:], scalar1=mxs[:, 2:3],
                                                         scalar2=None, op0=ALU.mult),
                     waits=[(V, v_rc)], inc=V)
        v_m1 = b.step("vector", lambda E: E.reduce_max(mxs[:, 0:1], rt[:], axis=AX.X),
                      waits=[(V, v_p)], inc=V)
        v_o1 = b.step("vector", lambda E: E.tensor_scalar(out=rt2[:], in0=rt[:], scalar1=mxs[:, 0:1],
                                                          scalar2=None, op0=ALU.is_ge),
                      waits=[(V, v_m1)], inc=V)
        v_t8 = b.step("vector", lambda E: E.scalar_tensor_tensor(out=scr[:, 0:8], in0=rt2[:], scalar=-2.0,
                                                                 in1=rt[:], op0=ALU.mult, op1=ALU.add),
                      waits=[(V, v_o1)], inc=V)
        v_m2 = b.step("vector", lambda E: E.reduce_max(mxs[:, 1:2], scr[:, 0:8], axis=AX.X),
                      waits=[(V, v_t8)], inc=V)
        v_o2 = b.step("vector", lambda E: E.tensor_scalar(out=scr[:, 0:8], in0=scr[:, 0:8],
                                                          scalar1=mxs[:, 1:2], scalar2=None, op0=ALU.is_ge),
                      waits=[(V, v_m2)], inc=V)
        v_dn = b.step("vector", lambda E: E.tensor_add(mxs[:, 2:3], mxs[:, 0:1], mxs[:, 1:2]),
                      waits=[(V, v_o2)], inc=V)
        v_rd = b.step("vector", lambda E: E.reciprocal(mxs[:, 3:4], mxs[:, 2:3]), waits=[(V, v_dn)], inc=V)
        v_w1 = b.step("vector", lambda E: E.tensor_mul(mxs[:, 0:1], mxs[:, 0:1], mxs[:, 3:4]),
                      waits=[(V, v_rd)], inc=V)
        v_w2 = b.step("vector", lambda E: E.tensor_mul(mxs[:, 1:2], mxs[:, 1:2], mxs[:, 3:4]),
                      waits=[(V, v_w1)], inc=V)
        v_me1 = b.step("vector", lambda E, i=i: E.tensor_scalar(out=me[:, i, :], in0=rt2[:],
                                                                scalar1=mxs[:, 0:1], scalar2=None,
                                                                op0=ALU.mult),
                       waits=[(V, v_w2)], inc=V)
        b.step("vector", lambda E, i=i: E.scalar_tensor_tensor(out=me[:, i, :], in0=scr[:, 0:8],
                                                               scalar=mxs[:, 1:2], in1=me[:, i, :],
                                                               op0=ALU.mult, op1=ALU.add),
               waits=[(V, v_me1)], inc=V)
    v_router_done = V.n

    t_gu_use = {0: None, 1: None}
    v_gu_witness = {0: None, 1: None}
    t_dn_use = None
    prev_aev = None
    prev_stt = {0: None, 1: None}
    s_gsb_prev = None
    for ei in range(9):
        shared = (ei == 8)
        dnw = []
        if prev_stt[0] is not None:
            dnw = [(V, prev_stt[0]), (V, prev_stt[1])]
        g_dn = b.dma("gpsimd", dn[:],
                     (shd_in if shared else ed_in[ei]).rearrange("(c p) d -> p c d", p=128),
                     waits=dnw, inc=G)
        for tg in range(4):
            aev = {}
            for gc in range(16):
                bi = gc % 2
                wl = [(V, v_gu_witness[bi])] if v_gu_witness[bi] is not None else []
                g_gu = b.dma("gpsimd", gu[:, bi, 0, :, :],
                             (shg_in if shared else eg_in[ei])[:, gc * 128:(gc + 1) * 128]
                             .rearrange("(c p) g -> p c g", p=128), waits=wl, inc=G)
                g_gu2 = b.dma("gpsimd", gu[:, bi, 1, :, :],
                              (shu_in if shared else eu_in[ei])[:, gc * 128:(gc + 1) * 128]
                              .rearrange("(c p) g -> p c g", p=128), inc=G)
                ggu = g_gu2
                def mm_g(E, gc=gc, bi=bi, tg=tg):
                    for c in range(8):
                        ins = E.matmul(pG[bi][:], gu[:, bi, 0, c, :], h2T[:, c, tg * 512:(tg + 1) * 512],
                                       start=(c == 0), stop=(c == 7))
                    return ins
                def mm_u(E, gc=gc, bi=bi, tg=tg):
                    for c in range(8):
                        ins = E.matmul(pU[bi][:], gu[:, bi, 1, c, :], h2T[:, c, tg * 512:(tg + 1) * 512],
                                       start=(c == 0), stop=(c == 7))
                    return ins
                w = [(G, ggu), (V, v_router_done)]
                if gc >= 2:
                    w.append((V, aev[gc - 2]))
                elif prev_aev is not None:
                    w.append((V, prev_aev))
                tg_mm = b.step("tensor", mm_g, waits=w, inc=T)
                tu_mm = b.step("tensor", mm_u, inc=T)
                t_gu_use[bi] = tu_mm
                ws = [(T, tu_mm)]
                if s_gsb_prev is not None:
                    ws.append((V, aev[gc - 1] if gc >= 1 else prev_aev))
                sg = b.step("scalar", lambda E, bi=bi: E.activation(out=gsb[:], in_=pG[bi][:],
                                                                    func=AF.Silu),
                            waits=ws, inc=S)
                s_gsb_prev = sg
                aev[gc] = b.step("vector", lambda E, gc=gc, bi=bi: E.tensor_mul(
                    aT[:, gc, :], pU[bi][:], gsb[:]), waits=[(S, sg)], inc=V)
                v_gu_witness[bi] = aev[gc]
            prev_aev = aev[15]
            for tt in range(4):
                i = tg * 4 + tt
                for cc in range(2):
                    def mm_d(E, tt=tt, cc=cc):
                        for gc in range(16):
                            ins = E.matmul(pD[cc][:], aT[:, gc, tt * 128:(tt + 1) * 128],
                                           dn[:, gc, cc * 512:(cc + 1) * 512],
                                           start=(gc == 0), stop=(gc == 15))
                        return ins
                    w = [(V, prev_aev), (G, g_dn)]
                    if prev_stt[cc] is not None:
                        w.append((V, prev_stt[cc]))
                    td = b.step("tensor", mm_d, waits=w, inc=T)
                    if shared:
                        prev_stt[cc] = b.step("vector", lambda E, i=i, cc=cc: E.scalar_tensor_tensor(
                            out=acc[:, i, cc * 512:(cc + 1) * 512], in0=pD[cc][:], scalar=1.0,
                            in1=acc[:, i, cc * 512:(cc + 1) * 512], op0=ALU.mult, op1=ALU.add),
                            waits=[(T, td)], inc=V)
                    else:
                        prev_stt[cc] = b.step("vector", lambda E, i=i, cc=cc, ei=ei: E.scalar_tensor_tensor(
                            out=acc[:, i, cc * 512:(cc + 1) * 512], in0=pD[cc][:], scalar=me[:, i, ei:ei + 1],
                            in1=acc[:, i, cc * 512:(cc + 1) * 512], op0=ALU.mult, op1=ALU.add),
                            waits=[(T, td)], inc=V)
        t_dn_use = T.n
    v_moe_done = V.n

    st_prev = None
    lx2 = {}
    for i in range(NT):
        lx2[i] = b.dma("sync", xbf[:, 0, :].bitcast(F32) if False else lnin[:, 0, :], xt_in[i],
                       waits=[(V, v_moe_done)] if i == 0 else [(V, vadd[i - 1])], inc=L)
        if i == 0:
            vadd = {}
        vadd[i] = b.step("vector", lambda E, i=i: E.tensor_add(acc[:, i, :], acc[:, i, :], lnin[:, 0, :]),
                         waits=[(L, lx2[i]), (V, v_moe_done)], inc=V)
        st_prev = emit_ln_store(b, acc, i, yt, 0, scr, stats, consts,
                                yt_out[i], sems,
                                extra_waits=[(V, vadd[i]), (L, l_consts)], st_prev=st_prev)
    b.wait("sync", [(ST, ST.n)])
    sc2.close()
    return b.build()


def host_inputs_pre(I, x_slice):
    return dict(
        x_in=np.ascontiguousarray(x_slice),
        wk_in=I["L1_ffn_wk"], wv_in=I["L1_ffn_wv"], n1_in=I["L1_ffn_norm1_w"],
        idt_in=np.eye(128, dtype=ml_dtypes.bfloat16),
        consts_in=np.tile(np.array([[1e-5, 1e-6, 0, 0]], np.float32), (128, 1)),
    )


def host_inputs_main(I, x_slice, kT_all, v_all):
    return dict(
        x_in=np.ascontiguousarray(x_slice), kT_all=kT_all, v_all=v_all,
        wq_in=I["L1_ffn_wq"], wo_in=I["L1_ffn_wo"],
        n1_in=I["L1_ffn_norm1_w"], n2_in=I["L1_ffn_norm2_w"],
        rw_in=I["L1_ffn_router_w"],
        shg_in=I["L1_ffn_sh_gate"], shu_in=I["L1_ffn_sh_up"], shd_in=I["L1_ffn_sh_down"],
        eg_in=I["L1_ffn_e_gate"], eu_in=I["L1_ffn_e_up"], ed_in=I["L1_ffn_e_down"],
        idt_in=np.eye(128, dtype=ml_dtypes.bfloat16),
        idtf_in=np.eye(128, dtype=np.float32),
        onesf_in=np.ones((1, 128), np.float32),
        consts_in=np.tile(np.array([[1e-5, 1e-6, 0, 0]], np.float32), (128, 1)),
    )


def assemble_kv(pre_results):
    kT_all = np.concatenate([r["kT_out"] for r in pre_results], axis=1)
    v_all = np.concatenate([r["v_out"] for r in pre_results], axis=0)
    return np.ascontiguousarray(kT_all), np.ascontiguousarray(v_all)




_NC_CACHE = {}


def _get(name, fn):
    if name not in _NC_CACHE:
        _NC_CACHE[name] = fn()
    return _NC_CACHE[name]


def kernel(**inputs):
    I = inputs
    x0 = np.asarray(I["L0_attn_x"])
    cores = list(range(8))

    ncA = _get("A", build_A)
    resA = run_bass_kernel_spmd(ncA, [host_inputs(I, c, x0) for c in cores], core_ids=cores)
    x1s = [resA.results[c]["x_out"] for c in cores]

    ncB = _get("B", build_B)
    resB = run_bass_kernel_spmd(ncB, [host_inputs_B(I, s) for s in x1s], core_ids=cores)
    x2s = [resB.results[c]["x_out"] for c in cores]

    ncC = _get("C", build_C)
    resC = run_bass_kernel_spmd(ncC, [host_inputs_C(I, s) for s in x2s], core_ids=cores)
    x3s = [resC.results[c]["x_out"] for c in cores]

    ncP = _get("P", build_Dpre)
    resP = run_bass_kernel_spmd(ncP, [host_inputs_pre(I, s) for s in x3s], core_ids=cores)
    kparts = [resP.results[c]["kT_out"] for c in cores]
    vparts = [resP.results[c]["v_out"] for c in cores]
    kT_all = np.ascontiguousarray(np.concatenate(
        [p[:, 0:1024] for p in kparts] + [p[:, 1024:] for p in kparts], axis=1))
    v_all = np.ascontiguousarray(np.concatenate(
        [p[0:1024] for p in vparts] + [p[1024:] for p in vparts], axis=0))
    ncM = _get("M", build_Dmain)
    resM = run_bass_kernel_spmd(ncM, [host_inputs_main(I, s, kT_all, v_all) for s in x3s],
                                core_ids=cores)
    return unshard_rows([resM.results[c]["x_out"] for c in cores])


# version 04 (final)
# speedup vs baseline: 1.2771x; 1.2771x over previous
import contextlib
import numpy as np
import ml_dtypes
import concourse.bass as bass
import concourse.mybir as mybir
from concourse.bass_utils import run_bass_kernel_spmd

"""Tiny raw-Bass emission framework: per-engine step queues + counted semaphores."""

F32 = mybir.dt.float32
BF16 = mybir.dt.bfloat16
U32 = mybir.dt.uint32
AF = mybir.ActivationFunctionType
ALU = mybir.AluOpType
AX = mybir.AxisListType

ENGINES = ("tensor", "vector", "scalar", "gpsimd", "sync")


class Sem:
    __slots__ = ("h", "n", "name")

    def __init__(self, h, name):
        self.h = h
        self.n = 0
        self.name = name

    def bump(self, k):
        self.n += k
        return self.n


class Builder:
    def __init__(self, name="k", paranoid=False, num_devices=None):
        self.nc = bass.Bass("TRN2", target_bir_lowering=False, name=name, num_devices=num_devices)
        self.q = {e: [] for e in ENGINES}
        self._es = contextlib.ExitStack()
        self._scopes = [self._es]
        self._nsem = 0
        self.paranoid = paranoid
        self._allsems = []
        self._last_inc = None
        self.prefix = ""
        self.alias = {}
        self.keep_external = None
        self._dram = {}

    def push_scope(self):
        sc = contextlib.ExitStack()
        self._scopes.append(sc)
        return sc

    def pop_scope(self):
        sc = self._scopes.pop()
        sc.close()

    def sem(self, name=None):
        name = self.prefix + (name or "s") + f"{self._nsem}"
        self._nsem += 1
        h = self._es.enter_context(self.nc.semaphore(name))
        s = Sem(h, name)
        self._allsems.append(s)
        return s

    def sbuf(self, name, shape, dtype, scope=None):
        es = scope if scope is not None else self._scopes[-1]
        return es.enter_context(self.nc.sbuf_tensor(self.prefix + name, shape, dtype))

    def scope(self):
        import contextlib as _cl
        return _cl.ExitStack()

    def psum(self, name, shape, dtype=F32, scope=None):
        es = scope if scope is not None else self._scopes[-1]
        return es.enter_context(self.nc.psum_tensor(self.prefix + name, shape, dtype))

    def din(self, name, shape, dtype=F32):
        full = self.alias.get(self.prefix + name, self.prefix + name)
        if full in self._dram:
            return self._dram[full]
        ap = self.nc.dram_tensor(full, shape, dtype, kind="ExternalInput").ap()
        self._dram[full] = ap
        return ap

    def dout(self, name, shape, dtype=F32):
        full = self.prefix + name
        kind = "ExternalOutput"
        if self.keep_external is not None and full not in self.keep_external:
            kind = "Internal"
        ap = self.nc.dram_tensor(full, shape, dtype, kind=kind).ap()
        self._dram[full] = ap
        return ap

    def dtmp(self, name, shape, dtype=F32, addr_space="Local"):
        full = self.prefix + name
        if full in self._dram:
            return self._dram[full]
        ap = self.nc.dram_tensor(full, shape, dtype, kind="Internal",
                                 addr_space=addr_space).ap()
        self._dram[full] = ap
        return ap

    def step(self, eng, fn, waits=(), inc=None, inc_k=None):
        if self.paranoid == "chain":
            extra = [self._last_inc] if self._last_inc is not None else []
            seen = {id(s) for (s, v) in extra}
            waits = extra + [(s, v) for (s, v) in waits if id(s) not in seen and v is not None]
        elif self.paranoid:
            cur = [(s, s.n) for s in self._allsems if s.n > 0]
            seen = {id(s) for (s, v) in cur}
            waits = cur + [(s, v) for (s, v) in waits if id(s) not in seen and v is not None]
        val = None
        if inc is not None:
            k = inc_k if inc_k is not None else 1
            val = inc.bump(k)
            if self.paranoid == "chain":
                self._last_inc = (inc, val)
        for (s, v) in waits:
            if v is None:
                import traceback
                traceback.print_stack()
                raise ValueError(f"None wait value on sem {s.name} for engine {eng}")
        wl = [(s.h, v) for (s, v) in waits]
        semh = inc.h if inc is not None else None
        kk = inc_k if inc_k is not None else 1

        def emit(E, wl=wl, fn=fn, semh=semh, kk=kk):
            for h, v in wl:
                E.wait_ge(h, v)
            ins = fn(E)
            if semh is not None:
                if ins is None:
                    ins = E.nop()
                ins.then_inc(semh, kk)

        self.q[eng].append(emit)
        return val

    def dma(self, eng, out, in_, waits=(), inc=None, **kw):
        return self.step(eng, lambda E: E.dma_start(out=out, in_=in_, **kw),
                         waits=waits, inc=inc, inc_k=16)

    def wait(self, eng, waits):
        self.step(eng, lambda E: None, waits=waits)

    def build(self):
        with self._es:
            with self.nc.Block() as blk:
                for eng in ENGINES:
                    steps = self.q[eng]
                    if not steps:
                        continue
                    getattr(blk, eng)(lambda E, steps=steps: [f(E) for f in steps])
        return self.nc


def shard_rows(x):
    out = []
    for c in range(8):
        out.append(np.ascontiguousarray(
            np.concatenate([x[0, 1024 * c:1024 * (c + 1)],
                            x[1, 1024 * c:1024 * (c + 1)]], axis=0)))
    return out


def unshard_rows(parts, D=1024):
    x = np.empty((2, 8192, D), parts[0].dtype)
    for c in range(8):
        x[0, 1024 * c:1024 * (c + 1)] = parts[c][:1024]
        x[1, 1024 * c:1024 * (c + 1)] = parts[c][1024:]
    return x


class TP:

    def __init__(self, b, pX, idt, T, V):
        self.b, self.pX, self.idt, self.T, self.V = b, pX, idt, T, V
        self.bank = 0
        self.last = {0: None, 1: None}

    def block8(self, src_fn, out_view, waits=()):
        b, T, V = self.b, self.T, self.V
        bi = self.bank
        self.bank ^= 1
        w = list(waits)
        single = self.pX[0] is self.pX[1]
        if single:
            prev = self.last[0] if self.last[0] is not None else self.last[1]
            if self.last[1] is not None and (prev is None or self.last[1] > prev):
                prev = self.last[1]
            if prev is not None:
                w.append((V, prev))
        elif self.last[bi] is not None:
            w.append((V, self.last[bi]))
        td = None
        for j in range(8):
            td = b.step("tensor",
                        lambda E, bi=bi, j=j: E.transpose(self.pX[bi][:, j * 128:(j + 1) * 128],
                                                          src_fn(j), self.idt[:]),
                        waits=(w if j == 0 else ()), inc=(T if j == 7 else None))
        S = getattr(self, "S", None)
        if S is not None:
            ev = b.step("scalar",
                        lambda E, bi=bi: E.copy(out_view,
                                                self.pX[bi][:].rearrange("p (j t) -> p j t", j=8)),
                        waits=[(T, td)], inc=S)
            self.last_S = ev
            self.last[bi] = None
            return ("S", ev)
        ev = b.step("vector",
                    lambda E, bi=bi: E.tensor_copy(out_view,
                                                   self.pX[bi][:].rearrange("p (j t) -> p j t", j=8)),
                    waits=[(T, td)], inc=V)
        self.last[bi] = ev
        return ev


"""Launch B: L0 ffn (dense top-1 MoE) + residual + LayerNorm.  x1_slice -> x2_slice."""

NT = 16


def emit_ln_store(b, acc, i, yt, bi, sq, stats, consts, dst, sems, extra_waits, st_prev):
    L, T, V, S, ST = sems
    s_q = b.step("scalar", lambda E, i=i: E.activation(out=sq[:], in_=acc[:, i, :], func=AF.Square,
                                                       accum_out=stats[:, 1:2]),
                 waits=extra_waits, inc=S)
    v_sx = b.step("vector", lambda E, i=i: E.reduce_sum(stats[:, 0:1], acc[:, i, :], axis=AX.X),
                  waits=extra_waits, inc=V)
    v_mean = b.step("vector", lambda E: E.tensor_scalar_mul(stats[:, 2:3], stats[:, 0:1], 1.0 / 1024.0),
                    waits=[(V, v_sx)], inc=V)
    v_v1 = b.step("vector", lambda E: E.tensor_scalar_mul(stats[:, 3:4], stats[:, 1:2], 1.0 / 1024.0),
                  waits=[(S, s_q)], inc=V)
    v_nm2 = b.step("vector", lambda E: E.tensor_scalar(out=stats[:, 4:5], in0=stats[:, 2:3],
                                                       scalar1=stats[:, 2:3], scalar2=-1.0,
                                                       op0=ALU.mult, op1=ALU.mult),
                   waits=[(V, v_mean)], inc=V)
    v_var = b.step("vector", lambda E: E.tensor_add(stats[:, 3:4], stats[:, 3:4], stats[:, 4:5]),
                   waits=[(V, v_v1), (V, v_nm2)], inc=V)
    s_sd = b.step("scalar", lambda E: E.activation(out=stats[:, 4:5], in_=stats[:, 3:4], func=AF.Sqrt,
                                                   bias=consts[:, 0:1], scale=1.0),
                  waits=[(V, v_var)], inc=S)
    v_rstd = b.step("vector", lambda E: E.reciprocal(stats[:, 5:6], stats[:, 4:5]), waits=[(S, s_sd)], inc=V)
    v_nm = b.step("vector", lambda E: E.tensor_scalar(out=stats[:, 2:3], in0=stats[:, 2:3],
                                                      scalar1=stats[:, 5:6], scalar2=-1.0,
                                                      op0=ALU.mult, op1=ALU.mult),
                  waits=[(V, v_rstd)], inc=V)
    wsto = [(ST, st_prev)] if st_prev is not None else []
    v_y = b.step("vector", lambda E, i=i, bi=bi: E.tensor_scalar(out=yt[:, bi, :], in0=acc[:, i, :],
                                                                 scalar1=stats[:, 5:6], scalar2=stats[:, 2:3],
                                                                 op0=ALU.mult, op1=ALU.add),
                 waits=[(V, v_nm)] + wsto, inc=V)
    return b.dma("sync", dst, yt[:, bi, :], waits=[(V, v_y)], inc=ST)


def emit_cast_transpose(b, src_tile_fn, n_tiles, xbf, XT, idt, pX, sems, src_waits):
    L, T, V, S, ST = sems
    tr_done, v_evac = {}, {}
    for i in range(n_tiles):
        bi = i % 2
        waits = list(src_waits(i))
        if i >= 2:
            waits.append((V, v_evac[i - 2]))
        cast_done = b.step("vector", lambda E, bi=bi, i=i: E.tensor_copy(xbf[:, bi, :], src_tile_fn(i)),
                           waits=waits, inc=V)
        tw = [(V, cast_done)]
        if i >= 2:
            tw.append((V, v_evac[i - 2]))
        for j in range(8):
            tr_done[i] = b.step("tensor",
                                lambda E, bi=bi, j=j: E.transpose(pX[bi][:, j * 128:(j + 1) * 128],
                                                                 xbf[:, bi, j * 128:(j + 1) * 128], idt[:]),
                                waits=(tw if j == 0 else ()), inc=(T if j == 7 else None))
        v_evac[i] = b.step("vector",
                           lambda E, bi=bi, i=i: E.tensor_copy(
                               XT[:, :, i * 128:(i + 1) * 128],
                               pX[bi][:].rearrange("p (j t) -> p j t", j=8)),
                           waits=[(T, tr_done[i])], inc=V)
    return v_evac


def build_B(b=None):
    if b is None:
        b = Builder("k", paranoid=True)
        b._standalone = True
    else:
        b._standalone = False
    x_in = b.din("x_in", [2048, 1024], F32)
    w_red_in = b.din("w_red_in", [1024, 16], F32)
    wg_in = b.din("wg_in", [8, 16], F32)
    W1_in = b.din("W1_in", [8, 1024, 2048], F32)
    b1_in = b.din("b1_in", [8, 2048], F32)
    W2_in = b.din("W2_in", [8, 2048, 1024], F32)
    b2_in = b.din("b2_in", [8, 1024], F32)
    idt_in = b.din("idt_in", [128, 128], BF16)
    ones_in = b.din("ones_in", [1, 512], BF16)
    consts_in = b.din("consts_in", [128, 4], F32)
    x_out = b.dout("x_out", [2048, 1024], F32)

    xt_in = x_in.rearrange("(i p) d -> i p d", p=128)
    yt_out = x_out.rearrange("(i p) d -> i p d", p=128)

    acc = b.sbuf("acc", [128, NT, 1024], F32)
    xbf = b.sbuf("xbf", [128, 2, 1024], BF16)
    XT = b.sbuf("XT", [128, 8, 2048], BF16)
    idt = b.sbuf("idt", [128, 128], BF16)
    ones = b.sbuf("ones", [1, 512], BF16)
    consts = b.sbuf("consts", [128, 4], F32)
    w_redb = b.sbuf("w_redb", [128, 8, 16], BF16)
    wg_sb = b.sbuf("wg_sb", [8, 16], F32)
    wgsq = b.sbuf("wgsq", [8, 16], F32)
    rn8 = b.sbuf("rn8", [8, 1], F32)
    wgn_bf = b.sbuf("wgn_bf", [8, 16], BF16)
    wgTn = b.sbuf("wgTn", [16, 8], BF16)
    xrT = b.sbuf("xrT", [16, 4, 512], BF16)
    lg = b.sbuf("lg", [128, 8], F32)
    m1 = b.sbuf("m1", [128, 1], F32)
    negm = b.sbuf("negm", [128, 1], F32)
    probs = b.sbuf("probs", [128, 8], F32)
    sums = b.sbuf("sums", [128, 1], F32)
    score = b.sbuf("score", [128, 1], F32)
    me = b.sbuf("me", [128, NT, 8], F32)
    W1b = b.sbuf("W1b", [128, 8, 2048], BF16)
    W2b = b.sbuf("W2b", [128, 16, 1024], BF16)
    b1sb = b.sbuf("b1sb", [128, 16], F32)
    b2row = b.sbuf("b2row", [1, 1024], BF16)
    hT = b.sbuf("hT", [128, 16, 512], BF16)
    sq = b.sbuf("sq", [128, 1024], F32)
    stats = b.sbuf("stats", [128, 6], F32)
    yt = b.sbuf("yt", [128, 2, 1024], F32)

    pX = [b.psum("pX0", [128, 1024], BF16), b.psum("pX1", [128, 1024], BF16)]
    pR = b.psum("pR", [128, 512], F32)
    pH = [b.psum("pH0", [128, 512], F32), b.psum("pH1", [128, 512], F32)]
    pO = [b.psum("pO0", [128, 512], F32), b.psum("pO1", [128, 512], F32)]
    pWT = b.psum("pWT", [128, 1024], BF16)

    L = b.sem("L"); G = b.sem("G"); T = b.sem("T"); V = b.sem("V"); S = b.sem("S"); ST = b.sem("ST")
    sems = (L, T, V, S, ST)

    lv = {}
    for i in range(NT):
        lv[i] = b.dma("sync", acc[:, i, :], xt_in[i], inc=L)
    l_idt = b.dma("sync", idt[:], idt_in, inc=L)
    l_ones = b.dma("sync", ones[:], ones_in, inc=L)
    l_consts = b.dma("sync", consts[:], consts_in, inc=L)
    l_wg = b.dma("sync", wg_sb[:], wg_in, inc=L)
    g_wred = b.dma("gpsimd", w_redb[:], w_red_in.rearrange("(c p) g -> p c g", p=128), inc=G)

    v_evac = emit_cast_transpose(b, lambda i: acc[:, i, :], NT, xbf, XT, idt, pX, sems,
                                 src_waits=lambda i: [(L, lv[i]), (L, l_idt)])

    s_sq = b.step("scalar", lambda E: E.activation(out=wgsq[:], in_=wg_sb[:], func=AF.Square),
                  waits=[(L, l_wg)], inc=S)
    v_rs = b.step("vector", lambda E: E.reduce_sum(rn8[:], wgsq[:], axis=AX.X), waits=[(S, s_sq)], inc=V)
    s_sqrt = b.step("scalar", lambda E: E.sqrt(rn8[:], rn8[:]), waits=[(V, v_rs)], inc=S)
    v_mx = b.step("vector", lambda E: E.tensor_scalar_max(rn8[:], rn8[:], 1e-4), waits=[(S, s_sqrt)], inc=V)
    v_rc = b.step("vector", lambda E: E.reciprocal(rn8[:], rn8[:]), waits=[(V, v_mx)], inc=V)
    v_wgn = b.step("vector", lambda E: E.tensor_scalar(out=wgn_bf[:], in0=wg_sb[:], scalar1=rn8[:],
                                                       scalar2=None, op0=ALU.mult),
                   waits=[(V, v_rc)], inc=V)
    t_wgT = b.step("tensor", lambda E: E.transpose(pWT[0:16, 0:8], wgn_bf[:], idt[0:8, 0:8]),
                   waits=[(V, v_wgn), (L, l_idt)], inc=T)
    v_wgT = b.step("vector", lambda E: E.tensor_copy(wgTn[:], pWT[0:16, 0:8]),
                   waits=[(T, t_wgT)], inc=V)

    xr_done = {}
    for tg in range(4):
        def mm_xr(E, tg=tg):
            for c in range(8):
                ins = E.matmul(pR[0:16, :], w_redb[:, c, :], XT[:, c, tg * 512:(tg + 1) * 512],
                               start=(c == 0), stop=(c == 7))
            return ins
        w = [(G, g_wred), (V, v_evac[tg * 4 + 3])]
        if tg > 0:
            w.append((V, xr_done[tg - 1]))
        tv = b.step("tensor", mm_xr, waits=w, inc=T)
        xr_done[tg] = b.step("vector", lambda E, tg=tg: E.tensor_copy(xrT[:, tg, :], pR[0:16, :]),
                             waits=[(T, tv)], inc=V)

    prev_vlg = xr_done[3]
    for i in range(NT):
        tg, tt = i // 4, i % 4
        t_lg = b.step("tensor",
                      lambda E, tg=tg, tt=tt: E.matmul(pR[:, 0:8], xrT[:, tg, tt * 128:(tt + 1) * 128],
                                                       wgTn[:], start=True, stop=True),
                      waits=[(V, prev_vlg), (V, v_wgT)], inc=T)
        v_lg = b.step("vector", lambda E: E.tensor_copy(lg[:], pR[:, 0:8]), waits=[(T, t_lg)], inc=V)
        prev_vlg = v_lg
        v_m1 = b.step("vector", lambda E: E.reduce_max(m1[:], lg[:], axis=AX.X), waits=[(V, v_lg)], inc=V)
        v_ng = b.step("vector", lambda E: E.tensor_scalar_mul(negm[:], m1[:], -1.0), waits=[(V, v_m1)], inc=V)
        s_ex = b.step("scalar", lambda E: E.activation(out=probs[:], in_=lg[:], func=AF.Exp,
                                                       bias=negm[:], scale=1.0, accum_out=sums[:]),
                      waits=[(V, v_ng)], inc=S)
        v_sc = b.step("vector", lambda E: E.reciprocal(score[:], sums[:]), waits=[(S, s_ex)], inc=V)
        v_oh = b.step("vector", lambda E: E.tensor_scalar(out=probs[:], in0=lg[:], scalar1=m1[:],
                                                          scalar2=None, op0=ALU.is_ge),
                      waits=[(V, v_sc)], inc=V)
        b.step("vector", lambda E, i=i: E.tensor_scalar(out=me[:, i, :], in0=probs[:], scalar1=score[:],
                                                        scalar2=None, op0=ALU.mult),
               waits=[(V, v_oh)], inc=V)

    t_last_W_use = T.n
    prev_stt = {0: None, 1: None}
    last_gelu = None
    for e in range(8):
        wts = []
        if e > 0:
            wts = [(S, last_gelu), (V, prev_stt[0]), (V, prev_stt[1])]
        g_w1 = b.dma("gpsimd", W1b[:], W1_in[e].rearrange("(c p) h -> p c h", p=128), waits=wts, inc=G)
        g_w2 = b.dma("gpsimd", W2b[:], W2_in[e].rearrange("(c p) d -> p c d", p=128), inc=G)
        g_b2 = b.dma("gpsimd", b2row[:], b2_in[e].unsqueeze(0), inc=G)
        l_b1 = b.dma("sync", b1sb[:], b1_in[e].rearrange("(c p) -> p c", p=128),
                     inc=L, allow_slow_non_contiguous=True)
        s_gelu = {}
        for tg in range(4):
            for hc in range(16):
                bk = hc % 2
                def mm_h(E, tg=tg, hc=hc, bk=bk):
                    for c in range(8):
                        ins = E.matmul(pH[bk][:], W1b[:, c, hc * 128:(hc + 1) * 128],
                                       XT[:, c, tg * 512:(tg + 1) * 512],
                                       start=(c == 0), stop=(c == 7))
                    return ins
                w = [(G, g_w1)]
                if (tg, hc) not in ((0, 0), (0, 1)):
                    prev = (tg, hc - 2) if hc >= 2 else (tg - 1, 14 + hc)
                    w.append((S, s_gelu[prev]))
                tmm = b.step("tensor", mm_h, waits=w, inc=T)
                s_gelu[(tg, hc)] = b.step("scalar",
                                          lambda E, hc=hc, bk=bk: E.activation(
                                              out=hT[:, hc, :], in_=pH[bk][:], func=AF.Gelu,
                                              bias=b1sb[:, hc:hc + 1], scale=1.0),
                                          waits=[(T, tmm), (L, l_b1)], inc=S)
            for tt in range(4):
                i = tg * 4 + tt
                for cc in range(2):
                    def mm_o(E, tt=tt, cc=cc):
                        E.matmul(pO[cc][:], ones[0:1, 0:128], b2row[:, cc * 512:(cc + 1) * 512],
                                 start=True, stop=False)
                        for hc in range(16):
                            ins = E.matmul(pO[cc][:], hT[:, hc, tt * 128:(tt + 1) * 128],
                                           W2b[:, hc, cc * 512:(cc + 1) * 512],
                                           start=False, stop=(hc == 15))
                        return ins
                    w = [(S, s_gelu[(tg, 15)]), (G, g_w2), (G, g_b2), (L, l_ones)]
                    if prev_stt[cc] is not None:
                        w.append((V, prev_stt[cc]))
                    tmo = b.step("tensor", mm_o, waits=w, inc=T)
                    prev_stt[cc] = b.step("vector",
                                          lambda E, i=i, cc=cc, e=e: E.scalar_tensor_tensor(
                                              out=acc[:, i, cc * 512:(cc + 1) * 512],
                                              in0=pO[cc][:], scalar=me[:, i, e:e + 1],
                                              in1=acc[:, i, cc * 512:(cc + 1) * 512],
                                              op0=ALU.mult, op1=ALU.add),
                                          waits=[(T, tmo)], inc=V)
        last_gelu = s_gelu[(3, 15)]
        t_last_W_use = T.n
    v_exp_done = V.n

    st_prev = {}
    for i in range(NT):
        sp = st_prev.get(i - 2)
        st_prev[i] = emit_ln_store(b, acc, i, yt, i % 2, sq, stats, consts, yt_out[i], sems,
                                   extra_waits=[(V, v_exp_done), (L, l_consts)], st_prev=sp)
    b.wait("sync", [(ST, ST.n)])
    return b.build() if b._standalone else None


def host_inputs_B(I, x1_slice):
    return dict(
        x_in=np.ascontiguousarray(x1_slice),
        w_red_in=I["L0_ffn_w_red"], wg_in=I["L0_ffn_wg"],
        W1_in=I["L0_ffn_W1"], b1_in=I["L0_ffn_b1"],
        W2_in=I["L0_ffn_W2"], b2_in=I["L0_ffn_b2"],
        idt_in=np.eye(128, dtype=ml_dtypes.bfloat16),
        ones_in=np.ones((1, 512), ml_dtypes.bfloat16),
        consts_in=np.tile(np.array([[1e-5, 0, 0, 0]], np.float32), (128, 1)),
    )




"""Launch C: L1 attn (delta-rule fast-weight, scan over the 2-batch axis)
+ residual + LayerNorm.  x2_slice -> x3_slice.
Row layout per core: tiles 0..7 = batch0 (t=0) seq chunk, tiles 8..15 = batch1 (t=1)."""

NT = 16


def build_C(b=None):
    if b is None:
        b = Builder("k", paranoid=True)
        b._standalone = True
    else:
        b._standalone = False
    x_in = b.din("x_in", [2048, 1024], F32)
    Wk_in = b.din("Wk_in", [1024, 1024], F32)
    Wv_in = b.din("Wv_in", [1024, 1024], F32)
    Wq_in = b.din("Wq_in", [1024, 1024], F32)
    idt_in = b.din("idt_in", [128, 128], BF16)
    consts_in = b.din("consts_in", [128, 4], F32)
    x_out = b.dout("x_out", [2048, 1024], F32)

    xt_in = x_in.rearrange("(i p) d -> i p d", p=128)
    yt_out = x_out.rearrange("(i p) d -> i p d", p=128)

    xf32 = b.sbuf("xf32", [128, 1, 1024], F32)
    xbf = b.sbuf("xbf", [128, 2, 1024], BF16)
    XT = b.sbuf("XT", [128, 8, 2048], BF16)
    idt = b.sbuf("idt", [128, 128], BF16)
    consts = b.sbuf("consts", [128, 4], F32)
    wraw = b.sbuf("wraw", [128, 2, 1024], BF16)
    WT = [b.sbuf(f"WT{w}", [128, 8, 1024], BF16) for w in range(3)]
    kvq = b.sbuf("kvq", [128, 6, 1024], F32)
    Sst = b.sbuf("Sst", [128, 8192], F32)
    tmp = b.sbuf("tmp", [128, 8192], F32)
    rec = b.sbuf("rec", [128, 3, 1024], F32)
    bsb0 = b.sbuf("bsb0", [128, 128], F32)
    stats = b.sbuf("stats", [128, 6], F32)
    yt = b.sbuf("yt", [128, 1, 1024], F32)
    x2pair = b.sbuf("x2pair", [128, 2, 1024], F32)

    pX = [b.psum("pX0", [128, 1024], BF16), b.psum("pX1", [128, 1024], BF16)]
    pA = [b.psum(f"pA{k}", [128, 512], F32) for k in range(4)]

    L = b.sem("L"); G = b.sem("G"); T = b.sem("T"); V = b.sem("V"); S = b.sem("S"); ST = b.sem("ST")
    sems = (L, T, V, S, ST)
    tp = TP(b, pX, idt, T, V)

    l_idt = b.dma("sync", idt[:], idt_in, inc=L)
    l_consts = b.dma("sync", consts[:], consts_in, inc=L)

    cast_of, lxv, ev_xt = {}, {}, {}
    for i in range(NT):
        bi = i % 2
        w = [] if i == 0 else [(V, cast_of[i - 1])]
        lxv[i] = b.dma("sync", xf32[:, 0, :], xt_in[i], waits=w, inc=L)
        prev_tr = [(V, ev_xt[i - 2])] if i >= 2 else []
        cast_of[i] = b.step("vector", lambda E, bi=bi: E.tensor_copy(xbf[:, bi, :], xf32[:, 0, :]),
                            waits=[(L, lxv[i])] + prev_tr, inc=V)
        ev_xt[i] = tp.block8(lambda j, bi=bi: xbf[:, bi, j * 128:(j + 1) * 128],
                             XT[:, :, i * 128:(i + 1) * 128],
                             waits=[(V, cast_of[i]), (L, l_idt)])
    v_xt_done = V.n

    W_ins = [Wk_in, Wv_in, Wq_in]
    wrow_ev = {}
    for w in range(3):
        for r in range(8):
            bi = r % 2
            gw = [] if (w, r) < (0, 2) else [(V, wrow_ev[(w, r - 2) if r >= 2 else (w - 1, 6 + r)])]
            g_w = b.dma("gpsimd", wraw[:, bi, :],
                        W_ins[w][r * 128:(r + 1) * 128, :], waits=gw, inc=G)
            wrow_ev[(w, r)] = tp.block8(
                lambda c, bi=bi: wraw[:, bi, c * 128:(c + 1) * 128],
                WT[w][:, :, r * 128:(r + 1) * 128],
                waits=[(G, g_w)])
    v_wt_done = V.n

    S4 = Sst[:].rearrange("p (n i j) -> p n i j", n=128, i=8)

    def v4i(x):
        return x.rearrange("p (n j) -> p n j", n=128).unsqueeze(2).broadcast_to([128, 128, 8, 8])

    def v4j(x):
        return x.rearrange("p (n i) -> p n i", n=128).unsqueeze(3).broadcast_to([128, 128, 8, 8])

    T4 = tmp[:].rearrange("p (n i j) -> p n i j", n=128, i=8)

    st_prev = {}
    prev_pair_done = None
    for jp in range(8):
        tiles = (jp, jp + 8)
        slot_ev = {}
        for wsel in range(3):
            for ti in range(2):
                slot = wsel * 2 + ti
                for nh in range(2):
                    bk = (2 * slot + nh) % 4
                    def mm1(E, wsel=wsel, bk=bk, nh=nh, tile=tiles[ti]):
                        for c in range(8):
                            ins = E.matmul(pA[bk][:], XT[:, c, tile * 128:(tile + 1) * 128],
                                           WT[wsel][:, c, nh * 512:(nh + 1) * 512],
                                           start=(c == 0), stop=(c == 7))
                        return ins
                    w = [(V, v_wt_done), (V, v_xt_done)]
                    if prev_pair_done is not None:
                        w.append((V, prev_pair_done))
                    prev = slot_ev.get((slot - 2, nh))
                    if prev is not None:
                        w.append((V, prev))
                    tm = b.step("tensor", mm1, waits=w, inc=T)
                    slot_ev[(slot, nh)] = b.step("vector",
                                                 lambda E, slot=slot, nh=nh, bk=bk: E.tensor_copy(
                                                     kvq[:, slot, nh * 512:(nh + 1) * 512], pA[bk][:]),
                                                 waits=[(T, tm)], inc=V)
        v_kvq = V.n
        lp = b.dma("sync", x2pair[:, 0, :], xt_in[tiles[0]],
                   waits=[(ST, st_prev[jp - 1])] if jp >= 1 else [], inc=L)
        lp2 = b.dma("sync", x2pair[:, 1, :], xt_in[tiles[1]], inc=L)

        k0, k1, v0, v1, q0, q1 = (kvq[:, s, :] for s in range(6))
        bs = bsb0[:]
        bsb = bs.rearrange("p (n o) -> p n o", n=128).broadcast_to([128, 128, 8])
        Sq = rec[:, 0, :]

        def knorm(kx, wv):
            s1 = b.step("scalar", lambda E: E.activation(out=tmp[:, 0:1024], in_=kx, func=AF.Square),
                        waits=[(V, wv)], inc=S)
            r1 = b.step("vector", lambda E: E.reduce_sum(
                bs.rearrange("p (n o) -> p n o", n=128),
                tmp[:, 0:1024].rearrange("p (n i) -> p n i", n=128), axis=AX.X),
                waits=[(S, s1)], inc=V)
            s2 = b.step("scalar", lambda E: E.sqrt(bs, bs), waits=[(V, r1)], inc=S)
            r2 = b.step("vector", lambda E: E.tensor_scalar_add(bs, bs, 1e-6), waits=[(S, s2)], inc=V)
            r3 = b.step("vector", lambda E: E.reciprocal(bs, bs), waits=[(V, r2)], inc=V)
            kx3 = kx.rearrange("p (n i) -> p n i", n=128)
            r4 = b.step("vector", lambda E: E.tensor_tensor(kx3, kx3, bsb, ALU.mult),
                        waits=[(V, r3)], inc=V)
            return r4

        vk0 = knorm(k0, v_kvq)
        v_s1 = b.step("vector", lambda E: E.tensor_tensor(S4, v4j(v0), v4i(k0), ALU.mult),
                      waits=[(V, vk0)], inc=V)
        s_th = b.step("scalar", lambda E: E.activation(out=Sst[:], in_=Sst[:], func=AF.Tanh),
                      waits=[(V, v_s1)], inc=S)
        v_sq = b.step("vector", lambda E: E.tensor_tensor(T4, S4, v4i(q0), ALU.mult),
                      waits=[(S, s_th)], inc=V)
        v_r = b.step("vector", lambda E: E.reduce_sum(Sq.unsqueeze(2), T4.rearrange("p n i j -> p (n i) j"), axis=AX.X),
                     waits=[(V, v_sq)], inc=V)
        s_si = b.step("scalar", lambda E: E.activation(out=rec[:, 1, :], in_=Sq, func=AF.Silu),
                      waits=[(V, v_r)], inc=S)
        v_o = b.step("vector", lambda E: E.tensor_mul(rec[:, 2, :], Sq, rec[:, 1, :]),
                     waits=[(S, s_si)], inc=V)
        v_ln0 = b.step("vector", lambda E: E.tensor_add(rec[:, 2, :], rec[:, 2, :], x2pair[:, 0, :]),
                       waits=[(V, v_o), (L, lp)], inc=V)
        st_a = emit_ln_store(b, rec, 2, yt, 0, tmp[:, 0:1024], stats, consts, yt_out[tiles[0]], sems,
                             extra_waits=[(V, v_ln0), (L, l_consts)], st_prev=st_prev.get(jp - 1))
        vk1 = knorm(k1, v_ln0)
        v_rt = b.step("vector", lambda E: E.tensor_tensor(T4, S4, v4i(k1), ALU.mult),
                      waits=[(V, vk1)], inc=V)
        v_r2 = b.step("vector", lambda E: E.reduce_sum(Sq.unsqueeze(2), T4.rearrange("p n i j -> p (n i) j"), axis=AX.X),
                      waits=[(V, v_rt)], inc=V)
        v_d = b.step("vector", lambda E: E.tensor_sub(v1, v1, Sq), waits=[(V, v_r2)], inc=V)
        v_op = b.step("vector", lambda E: E.tensor_tensor(T4, v4j(v1), v4i(k1), ALU.mult),
                      waits=[(V, v_d)], inc=V)
        v_s2 = b.step("vector", lambda E: E.tensor_add(Sst[:], Sst[:], tmp[:]), waits=[(V, v_op)], inc=V)
        s_th2 = b.step("scalar", lambda E: E.activation(out=Sst[:], in_=Sst[:], func=AF.Tanh),
                       waits=[(V, v_s2)], inc=S)
        v_sq2 = b.step("vector", lambda E: E.tensor_tensor(T4, S4, v4i(q1), ALU.mult),
                       waits=[(S, s_th2)], inc=V)
        v_r3 = b.step("vector", lambda E: E.reduce_sum(Sq.unsqueeze(2), T4.rearrange("p n i j -> p (n i) j"), axis=AX.X),
                      waits=[(V, v_sq2)], inc=V)
        s_si2 = b.step("scalar", lambda E: E.activation(out=rec[:, 1, :], in_=Sq, func=AF.Silu),
                       waits=[(V, v_r3)], inc=S)
        v_o2 = b.step("vector", lambda E: E.tensor_mul(rec[:, 2, :], Sq, rec[:, 1, :]),
                      waits=[(S, s_si2)], inc=V)
        v_ln1 = b.step("vector", lambda E: E.tensor_add(rec[:, 2, :], rec[:, 2, :], x2pair[:, 1, :]),
                       waits=[(V, v_o2), (L, lp2)], inc=V)
        st_prev[jp] = emit_ln_store(b, rec, 2, yt, 0, tmp[:, 0:1024], stats, consts, yt_out[tiles[1]], sems,
                                    extra_waits=[(V, v_ln1), (L, l_consts)], st_prev=st_a)
        prev_pair_done = V.n
    b.wait("sync", [(ST, ST.n)])
    return b.build() if b._standalone else None


def host_inputs_C(I, x_slice):
    return dict(
        x_in=np.ascontiguousarray(x_slice),
        Wk_in=I["L1_attn_W_k"], Wv_in=I["L1_attn_W_v"], Wq_in=I["L1_attn_W_q"],
        idt_in=np.eye(128, dtype=ml_dtypes.bfloat16),
        consts_in=np.tile(np.array([[1e-5, 0, 0, 0]], np.float32), (128, 1)),
    )




"""Launch A: L0 MLA causal attention + residual + LayerNorm.  x0 -> x1_slice.
v1: latents computed replicated from x_full; dense-padded causal via mask-matmul;
2-pass exact softmax with PSUM-resident scores."""

SCALE_A = float(128 ** -0.5)
NT = 16


def build_A(b=None):
    if b is None:
        b = Builder("k", paranoid=True)
        b._standalone = True
    else:
        b._standalone = False
    x_full = b.din("x_full", [2, 8192, 1024], F32)
    x_own = b.din("x_own", [2048, 1024], F32)
    wq_in = b.din("wq_in", [1024, 1024], F32)
    wkd_in = b.din("wkd_in", [1024, 64], F32)
    wvd_in = b.din("wvd_in", [1024, 64], F32)
    wku_in = b.din("wku_in", [64, 128], F32)
    wvu_in = b.din("wvu_in", [64, 128], F32)
    wo_in = b.din("wo_in", [1024, 1024], F32)
    cm_in = b.din("cm_in", [8, 16, 128, 512], BF16)
    idt_in = b.din("idt_in", [128, 128], BF16)
    idtf_in = b.din("idtf_in", [128, 128], F32)
    onesf_in = b.din("onesf_in", [1, 128], F32)
    consts_in = b.din("consts_in", [128, 4], F32)
    x_out = b.dout("x_out", [2048, 1024], F32)

    xo_t = x_own.rearrange("(i p) d -> i p d", p=128)
    yt_out = x_out.rearrange("(i p) d -> i p d", p=128)

    klatT = b.sbuf("klatT", [64, 2, 8192], BF16)
    vlatT = b.sbuf("vlatT", [64, 2, 8192], BF16)
    QT = b.sbuf("QT", [128, 8, 1024], BF16)
    KT = b.sbuf("KT", [128, 8192], BF16)
    Vtm = b.sbuf("Vtm", [128, 64, 128], BF16)
    cmask = b.sbuf("cmask", [128, 16, 512], BF16)
    YT = b.sbuf("YT", [128, 8, 1024], BF16)
    P = b.sbuf("P", [128, 2, 512], BF16)
    PT = b.sbuf("PT", [128, 64, 128], BF16)
    rb = b.sbuf("rb", [128, 128], F32)
    rrow = b.sbuf("rrow", [1, 128], F32)
    xbf = b.sbuf("xbf", [128, 2, 1024], BF16)
    XTt = b.sbuf("XTt", [128, 2, 8, 128], BF16)
    wbig = b.sbuf("wbig", [128, 8, 1024], BF16)
    wkdb = b.sbuf("wkdb", [128, 8, 64], BF16)
    wvdb = b.sbuf("wvdb", [128, 8, 64], BF16)
    wkub = b.sbuf("wkub", [64, 128], BF16)
    wvub = b.sbuf("wvub", [64, 128], BF16)
    idt = b.sbuf("idt", [128, 128], BF16)
    idtf = b.sbuf("idtf", [128, 128], F32)
    onesf = b.sbuf("onesf", [1, 128], F32)
    consts = b.sbuf("consts", [128, 4], F32)
    cmx = b.sbuf("cmx", [128, 16], F32)
    sums_c = b.sbuf("sums_c", [128, 16], F32)
    mxs = b.sbuf("mxs", [128, 4], F32)
    xown = b.sbuf("xown", [128, 1024], F32)
    lnin = b.sbuf("lnin", [128, 1, 1024], F32)
    stats = b.sbuf("stats", [128, 6], F32)
    yt = b.sbuf("yt", [128, 1, 1024], F32)

    pX = [b.psum("pX0", [128, 1024], BF16), b.psum("pX1", [128, 1024], BF16)]
    pS = [b.psum("pS0", [128, 512], F32), b.psum("pS1", [128, 512], F32)]
    pT_ = b.psum("pT", [128, 1024], BF16)
    pO = b.psum("pO", [128, 512], F32)
    pW = [b.psum("pW0", [128, 512], F32), b.psum("pW1", [128, 512], F32)]

    L = b.sem("L"); G = b.sem("G"); T = b.sem("T"); V = b.sem("V"); S = b.sem("S"); ST = b.sem("ST")
    sems = (L, T, V, S, ST)
    tp = TP(b, pX, idt, T, V)

    l_idt = b.dma("sync", idt[:], idt_in, inc=L)
    l_idtf = b.dma("sync", idtf[:], idtf_in, inc=L)
    l_onesf = b.dma("sync", onesf[:], onesf_in, inc=L)
    l_consts = b.dma("sync", consts[:], consts_in, inc=L)
    g_wq = b.dma("gpsimd", wbig[:], wq_in.rearrange("(c p) d -> p c d", p=128), inc=G)
    g_wkd = b.dma("gpsimd", wkdb[:], wkd_in.rearrange("(c p) d -> p c d", p=128), inc=G)
    g_wvd = b.dma("gpsimd", wvdb[:], wvd_in.rearrange("(c p) d -> p c d", p=128), inc=G)
    g_wku = b.dma("gpsimd", wkub[:], wku_in, inc=G)
    g_wvu = b.dma("gpsimd", wvub[:], wvu_in, inc=G)

    prev_latev = None
    xload = {}
    for i in range(128):
        bb_, col = i // 64, (i % 64) * 128
        w = [(V, tp.last[i % 2])] if i >= 2 and tp.last[i % 2] is not None else []
        xload[i] = b.dma("gpsimd", xbf[:, i % 2, :],
                         x_full[bb_, col:col + 128, :], waits=w, inc=G)
        ev = tp.block8(lambda j, bi=i % 2: xbf[:, bi, j * 128:(j + 1) * 128],
                       XTt[:, i % 2, :, :], waits=[(G, xload[i]), (L, l_idt)])
        w0 = [(V, ev), (G, g_wkd), (G, g_wvd)]
        if prev_latev is not None:
            w0.append((V, prev_latev))
        tk = b.step("tensor", lambda E, bi=i % 2: [E.matmul(pS[0][0:64, 0:128], wkdb[:, c, :],
                                                            XTt[:, bi, c, :], start=(c == 0), stop=(c == 7))
                                                   for c in range(8)][-1], waits=w0, inc=T)
        tv = b.step("tensor", lambda E, bi=i % 2: [E.matmul(pS[1][0:64, 0:128], wvdb[:, c, :],
                                                            XTt[:, bi, c, :], start=(c == 0), stop=(c == 7))
                                                   for c in range(8)][-1], inc=T)
        e1 = b.step("vector", lambda E, bb_=bb_, col=col: E.tensor_copy(
            klatT[:, bb_, col:col + 128], pS[0][0:64, 0:128]), waits=[(T, tk)], inc=V)
        prev_latev = b.step("vector", lambda E, bb_=bb_, col=col: E.tensor_copy(
            vlatT[:, bb_, col:col + 128], pS[1][0:64, 0:128]), waits=[(T, tv)], inc=V)

    g_wo = None
    prev_qev = prev_latev
    t_prev_kv_use = T.n
    v_ps_guard = V.n
    t_prev_mask_use = T.n
    v_prev_yt = None
    s_prev_exp = None
    v_prev_wo = {0: None, 1: None}
    st_prev = None
    v_prev_lat_use = None
    for bb_ in range(2):
        if bb_ == 1:
            g_wq = b.dma("gpsimd", wbig[:], wq_in.rearrange("(c p) d -> p c d", p=128),
                         waits=[(V, v_prev_wo[1])], inc=G)
        for i in range(bb_ * 8, bb_ * 8 + 8):
            col = (i % 8) * 128
            w = [(V, tp.last[i % 2])] if tp.last[i % 2] is not None else []
            gx = b.dma("gpsimd", xbf[:, i % 2, :], xo_t[i], waits=w, inc=G)
            ev = tp.block8(lambda j, bi=i % 2: xbf[:, bi, j * 128:(j + 1) * 128],
                           XTt[:, i % 2, :, :], waits=[(G, gx)])
            for h in range(8):
                bk = h % 2
                def mm_q(E, bi=i % 2, h=h, bk=bk):
                    for c in range(8):
                        ins = E.matmul(pS[bk][:, 0:128], wbig[:, c, h * 128:(h + 1) * 128],
                                       XTt[:, bi, c, :], start=(c == 0), stop=(c == 7))
                    return ins
                w1 = [(V, ev), (G, g_wq)]
                if prev_qev is not None:
                    w1.append((V, prev_qev))
                tq = b.step("tensor", mm_q, waits=w1, inc=T)
                prev_qev = b.step("vector", lambda E, h=h, bb_=bb_, col=col, bk=bk: E.tensor_copy(
                    QT[:, h, col:col + 128], pS[bk][:, 0:128]),
                    waits=[(T, tq)], inc=V)

        g_wo = b.dma("gpsimd", wbig[:], wo_in.rearrange("(c p) d -> p c d", p=128),
                     waits=[(V, prev_qev)], inc=G)
        v_ps_guard = prev_qev
        for kv in range(2):
            exp_ev = []
            for kc in range(16):
                def mm_kt(E, kc=kc, bb_=bb_, kv=kv):
                    return E.matmul(pS[kc % 2][:], wkub[kv * 32:(kv + 1) * 32, :], klatT[kv * 32:(kv + 1) * 32, bb_, kc * 512:(kc + 1) * 512],
                                    start=True, stop=True)
                w2 = [(G, g_wku)]
                if kc < 2:
                    w2.append((T, t_prev_kv_use))
                    w2.append((V, v_ps_guard))
                    if s_prev_exp is not None:
                        w2.append((S, s_prev_exp))
                    if v_prev_yt is not None:
                        w2.append((V, v_prev_yt))
                else:
                    w2.append((V, exp_ev[kc - 2]))
                tm = b.step("tensor", mm_kt, waits=w2, inc=T)
                exp_ev.append(b.step("vector", lambda E, kc=kc: E.tensor_copy(
                    KT[:, kc * 512:(kc + 1) * 512], pS[kc % 2][:]), waits=[(T, tm)], inc=V))
            vexp = {}
            for g4 in range(16):
                bk = g4 % 2
                def mm_v(E, g4=g4, bb_=bb_, kv=kv, bk=bk):
                    for j in range(4):
                        kt = g4 * 4 + j
                        ins = E.matmul(pS[bk][:, j * 128:(j + 1) * 128],
                                       vlatT[kv * 32:(kv + 1) * 32, bb_, kt * 128:(kt + 1) * 128],
                                       wvub[kv * 32:(kv + 1) * 32, :], start=True, stop=True)
                    return ins
                w3 = [(G, g_wvu), (V, exp_ev[14 + (g4 % 2)] if g4 < 2 else vexp[g4 - 2])]
                if g4 < 2 and v_prev_yt is not None:
                    w3.append((V, v_prev_yt))
                tm = b.step("tensor", mm_v, waits=w3, inc=T)
                vexp[g4] = b.step("vector", lambda E, g4=g4, bk=bk: E.tensor_copy(
                    Vtm[:, g4 * 4:(g4 + 1) * 4, :].rearrange("p a d -> p (a d)"), pS[bk][:]),
                    waits=[(T, tm)], inc=V)
            v_kv_ready = V.n
            for tt in range(8):
                lmw = [(T, t_prev_mask_use)]
                if s_prev_exp is not None:
                    lmw.append((S, s_prev_exp))
                lm = b.dma("sync", cmask[:], cm_in[tt].rearrange("k p j -> p k j"),
                           waits=lmw, inc=L)
                for rep in range(4):
                    h = kv * 4 + rep
                    qsl = QT[:, h, tt * 128:(tt + 1) * 128]
                    red = {}
                    for kc in range(16):
                        bk = kc % 2
                        def mm_s(E, kc=kc, bk=bk, qsl=qsl):
                            E.matmul(pS[bk][:], qsl, KT[:, kc * 512:(kc + 1) * 512], start=True, stop=False)
                            return E.matmul(pS[bk][:], idt[:], cmask[:, kc, :], start=False, stop=True)
                        w4 = [(L, lm), (V, v_kv_ready)]
                        if kc >= 2:
                            w4.append((V, red[kc - 2]))
                        elif s_prev_exp is not None:
                            w4.append((S, s_prev_exp))
                        tm = b.step("tensor", mm_s, waits=w4, inc=T)
                        red[kc] = b.step("vector", lambda E, kc=kc, bk=bk: E.reduce_max(
                            cmx[:, kc:kc + 1], pS[bk][:], axis=AX.X), waits=[(T, tm)], inc=V)
                    v_mx = b.step("vector", lambda E: E.reduce_max(mxs[:, 0:1], cmx[:], axis=AX.X),
                                  waits=[(V, red[15])], inc=V)
                    v_ng = b.step("vector", lambda E: E.tensor_scalar_mul(mxs[:, 1:2], mxs[:, 0:1], -SCALE_A),
                                  waits=[(V, v_mx)], inc=V)
                    s_exp, tr_of, ptev, pv_of = {}, {}, {}, {}
                    for kc in range(16):
                        bk = kc % 2
                        def mm_s2(E, kc=kc, bk=bk, qsl=qsl):
                            E.matmul(pS[bk][:], qsl, KT[:, kc * 512:(kc + 1) * 512], start=True, stop=False)
                            return E.matmul(pS[bk][:], idt[:], cmask[:, kc, :], start=False, stop=True)
                        w5 = [(V, v_ng)]
                        if kc >= 2:
                            w5.append((S, s_exp[kc - 2]))
                        tm = b.step("tensor", mm_s2, waits=w5, inc=T)
                        we = [(T, tm)]
                        if kc >= 2:
                            we.append((T, tr_of[kc - 2]))
                        s_exp[kc] = b.step("scalar", lambda E, kc=kc, bk=bk: E.activation(
                            out=P[:, kc % 2, :], in_=pS[bk][:], func=AF.Exp,
                            bias=mxs[:, 1:2], scale=SCALE_A, accum_out=sums_c[:, kc:kc + 1]),
                            waits=we, inc=S)
                        wtr = [(S, s_exp[kc])]
                        if kc >= 1:
                            wtr.append((V, ptev[kc - 1]))
                        for j in range(4):
                            tr_of[kc] = b.step("tensor", lambda E, kc=kc, j=j: E.transpose(
                                pT_[:, j * 128:(j + 1) * 128],
                                P[:, kc % 2, j * 128:(j + 1) * 128], idt[:]),
                                waits=(wtr if j == 0 else ()), inc=(T if j == 3 else None))
                        wpt = [(T, tr_of[kc])]
                        if kc == 0 and v_prev_yt is not None:
                            wpt.append((V, v_prev_yt))
                        ptev[kc] = b.step("vector", lambda E, kc=kc: E.tensor_copy(
                            PT[:, kc * 4:(kc + 1) * 4, :].rearrange("p a d -> p (a d)"),
                            pT_[:, 0:512]), waits=wpt, inc=V)
                    s_prev_exp = s_exp[15]
                    def mm_pv(E):
                        for kt in range(64):
                            ins = E.matmul(pO[:, 0:128], Vtm[:, kt, :], PT[:, kt, :],
                                           start=(kt == 0), stop=(kt == 63))
                        return ins
                    w6 = [(V, ptev[15])]
                    if v_prev_yt is not None:
                        w6.append((V, v_prev_yt))
                    pv_all = b.step("tensor", mm_pv, waits=w6, inc=T)
                    pv_of[15] = pv_all
                    v_sm = b.step("vector", lambda E: E.reduce_sum(mxs[:, 2:3], sums_c[:], axis=AX.X),
                                  waits=[(S, s_exp[15])], inc=V)
                    v_rc = b.step("vector", lambda E: E.reciprocal(mxs[:, 3:4], mxs[:, 2:3]),
                                  waits=[(V, v_sm)], inc=V)
                    t_tr = b.step("tensor", lambda E: E.transpose(pO[0:1, 256:384], mxs[:, 3:4], idtf[:]),
                                  waits=[(V, v_rc), (T, pv_of[15]), (L, l_idtf)], inc=T)
                    v_rr = b.step("vector", lambda E: E.tensor_copy(rrow[:], pO[0:1, 256:384]),
                                  waits=[(T, t_tr)], inc=V)
                    t_bc = b.step("tensor", lambda E: E.matmul(pO[:, 128:256], onesf[:], rrow[:],
                                                               start=True, stop=True),
                                  waits=[(L, l_onesf), (V, v_rr)], inc=T)
                    v_rb = b.step("vector", lambda E: E.tensor_copy(rb[:], pO[:, 128:256]),
                                  waits=[(T, t_bc)], inc=V)
                    v_prev_yt = b.step("vector", lambda E, h=h, tt=tt: E.tensor_tensor(
                        YT[:, h, tt * 128:(tt + 1) * 128], pO[:, 0:128], rb[:], ALU.mult),
                        waits=[(V, v_rb)], inc=V)
                t_prev_mask_use = T.n
            t_prev_kv_use = T.n
        for tt in range(8):
            i = bb_ * 8 + tt
            lxo = b.dma("sync", xown[:], xo_t[i],
                        waits=([(ST, st_prev)] if st_prev is not None else []), inc=L)
            for cc in range(2):
                def mm_wo(E, tt=tt, cc=cc):
                    for h in range(8):
                        ins = E.matmul(pW[cc][:], YT[:, h, tt * 128:(tt + 1) * 128],
                                       wbig[:, h, cc * 512:(cc + 1) * 512],
                                       start=(h == 0), stop=(h == 7))
                    return ins
                w7 = [(G, g_wo), (V, v_prev_yt)]
                if v_prev_wo[cc] is not None:
                    w7.append((V, v_prev_wo[cc]))
                tw = b.step("tensor", mm_wo, waits=w7, inc=T)
                v_prev_wo[cc] = b.step("vector", lambda E, cc=cc: E.scalar_tensor_tensor(
                    out=lnin[:, 0, cc * 512:(cc + 1) * 512], in0=pW[cc][:], scalar=1.0,
                    in1=xown[:, cc * 512:(cc + 1) * 512], op0=ALU.mult, op1=ALU.add),
                    waits=[(T, tw), (L, lxo)], inc=V)
            st_prev = emit_ln_store(b, lnin, 0, yt, 0, xown, stats, consts, yt_out[i], sems,
                                    extra_waits=[(V, v_prev_wo[0]), (V, v_prev_wo[1]), (L, l_consts)],
                                    st_prev=st_prev)
    b.wait("sync", [(ST, ST.n)])
    return b.build() if b._standalone else None


def host_inputs(I, core, x_full):
    cm = np.empty((8, 16, 128, 512), np.float32)
    q0s = 1024 * core + 128 * np.arange(8)
    for tt in range(8):
        qpos = q0s[tt] + np.arange(128)[:, None]
        for kc in range(16):
            kpos = kc * 512 + np.arange(512)[None, :]
            cm[tt, kc] = np.where(kpos <= qpos, 0.0, -3.0e38)
    xo = np.concatenate([x_full[0, 1024 * core:1024 * (core + 1)],
                         x_full[1, 1024 * core:1024 * (core + 1)]], axis=0)
    return dict(
        x_full=x_full, x_own=np.ascontiguousarray(xo),
        wq_in=I["L0_attn_wq"], wkd_in=I["L0_attn_wk_down"], wvd_in=I["L0_attn_wv_down"],
        wku_in=np.concatenate([I["L0_attn_wk_up"]]*2, 0), wvu_in=np.concatenate([I["L0_attn_wv_up"]]*2, 0), wo_in=I["L0_attn_wo"],
        cm_in=cm.astype(ml_dtypes.bfloat16),
        idt_in=np.eye(128, dtype=ml_dtypes.bfloat16),
        idtf_in=np.eye(128, dtype=np.float32),
        onesf_in=np.ones((1, 128), np.float32),
        consts_in=np.tile(np.array([[1e-5, 0, 0, 0]], np.float32), (128, 1)),
    )




"""Launch D (two parts): L1 ffn block (pre-norm MHA + top-2 MoE) + residual + final LN.
D-pre:  x3_slice -> own kT/v slices (bf16).
host:   concatenate -> kT_all [1024, 16384], v_all [16384, 1024].
D-main: x3_slice + kT_all + v_all + weights -> x4_slice."""

SCALE_D = 0.125
NT = 16
RSQ = 1.0 / 1024.0


def emit_rms_norm_tile(b, sems, src_ap, scr, stats, consts_eps_col, xnbf_dst, waits):
    L, T, V, S, ST = sems
    s1 = b.step("scalar", lambda E: E.activation(out=scr[:], in_=src_ap, func=AF.Square,
                                                 accum_out=stats[:, 0:1]), waits=waits, inc=S)
    v1 = b.step("vector", lambda E: E.tensor_scalar_mul(stats[:, 1:2], stats[:, 0:1], RSQ),
                waits=[(S, s1)], inc=V)
    s2 = b.step("scalar", lambda E: E.activation(out=stats[:, 2:3], in_=stats[:, 1:2], func=AF.Sqrt,
                                                 bias=consts_eps_col, scale=1.0),
                waits=[(V, v1)], inc=S)
    v2 = b.step("vector", lambda E: E.reciprocal(stats[:, 3:4], stats[:, 2:3]), waits=[(S, s2)], inc=V)
    v3 = b.step("vector", lambda E: E.tensor_scalar(out=xnbf_dst, in0=src_ap, scalar1=stats[:, 3:4],
                                                    scalar2=None, op0=ALU.mult),
                waits=[(V, v2)], inc=V)
    return v3


def build_Dpre(b=None):
    if b is None:
        b = Builder("k", paranoid=True)
        b._standalone = True
    else:
        b._standalone = False
    x_in = b.din("x_in", [2048, 1024], F32)
    wk_in = b.din("wk_in", [1024, 1024], F32)
    wv_in = b.din("wv_in", [1024, 1024], F32)
    n1_in = b.din("n1_in", [1024], F32)
    idt_in = b.din("idt_in", [128, 128], BF16)
    consts_in = b.din("consts_in", [128, 4], F32)
    kT_out = b.dout("kT_out", [1024, 2048], BF16)
    vslc = b.dout("vslc", [2048, 1024], BF16)

    xt_in = x_in.rearrange("(i p) d -> i p d", p=128)
    kT_o = kT_out.rearrange("(c p) t -> p c t", p=128)
    v_o = vslc.rearrange("(i p) d -> i p d", p=128)

    xf32 = b.sbuf("xf32", [128, 2, 1024], F32)
    xbf = b.sbuf("xbf", [128, 2, 1024], BF16)
    XTt = b.sbuf("XTt", [128, 2, 8, 128], BF16)
    scr = b.sbuf("scr", [128, 1024], F32)
    stats = b.sbuf("stats", [128, 6], F32)
    idt = b.sbuf("idt", [128, 128], BF16)
    consts = b.sbuf("consts", [128, 4], F32)
    w1sb = b.sbuf("w1sb", [128, 8], F32)
    wkb = b.sbuf("wkb", [128, 8, 1024], BF16)
    wvb = b.sbuf("wvb", [128, 8, 1024], BF16)
    ktile = b.sbuf("ktile", [128, 8, 128], BF16)
    vtile = b.sbuf("vtile", [128, 1024], BF16)

    pX = [b.psum("pX0", [128, 1024], BF16), b.psum("pX1", [128, 1024], BF16)]
    pK = [b.psum("pK0", [128, 512], F32), b.psum("pK1", [128, 512], F32)]
    pV = [b.psum("pV0", [128, 512], F32), b.psum("pV1", [128, 512], F32)]

    L = b.sem("L"); G = b.sem("G"); T = b.sem("T"); V = b.sem("V"); S = b.sem("S"); ST = b.sem("ST")
    sems = (L, T, V, S, ST)
    tp = TP(b, pX, idt, T, V)

    l_idt = b.dma("sync", idt[:], idt_in, inc=L)
    l_consts = b.dma("sync", consts[:], consts_in, inc=L)
    l_w1 = b.dma("sync", w1sb[:], n1_in.rearrange("(c p) -> p c", p=128), inc=L,
                 allow_slow_non_contiguous=True)
    g_wk = b.dma("gpsimd", wkb[:], wk_in.rearrange("(c p) d -> p c d", p=128), inc=G)
    g_wv = b.dma("gpsimd", wvb[:], wv_in.rearrange("(c p) d -> p c d", p=128), inc=G)

    st_k, st_v = {}, {}
    prev_kev = None
    prev_vev = None
    for i in range(NT):
        bi = i % 2
        wl = [] if i < 2 else [(V, norm_of[i - 2])]
        lx = b.dma("sync", xf32[:, bi, :], xt_in[i], waits=wl, inc=L)
        if i == 0:
            norm_of = {}
        norm_of[i] = emit_rms_norm_tile(b, sems, xf32[:, bi, :], scr, stats, consts[:, 1:2],
                                        xbf[:, bi, :],
                                        waits=[(L, lx), (L, l_consts)] +
                                              ([(V, ev_of[i - 2])] if i >= 2 else []))
        if i == 0:
            ev_of = {}
        ev_of[i] = tp.block8(lambda j, bi=bi: xbf[:, bi, j * 128:(j + 1) * 128],
                             XTt[:, bi, :, :], waits=[(V, norm_of[i]), (L, l_idt)])
        hv = b.step("vector", lambda E, bi=bi: E.tensor_tensor(
            XTt[:, bi, :, :], XTt[:, bi, :, :],
            w1sb[:].unsqueeze(2).broadcast_to([128, 8, 128]), ALU.mult),
            waits=[(V, ev_of[i]), (L, l_w1)], inc=V)
        kev = {}
        for f in range(8):
            bk = f % 2
            def mm_k(E, bi=bi, f=f, bk=bk):
                for c in range(8):
                    ins = E.matmul(pK[bk][:, 0:128], wkb[:, c, f * 128:(f + 1) * 128],
                                   XTt[:, bi, c, :], start=(c == 0), stop=(c == 7))
                return ins
            w = [(V, hv), (G, g_wk)]
            if f >= 2:
                w.append((V, kev[f - 2]))
            elif prev_kev is not None:
                w.append((V, prev_kev))
            tm = b.step("tensor", mm_k, waits=w, inc=T)
            kw = [(T, tm)]
            if f == 0 and i >= 1:
                kw.append((ST, st_k[i - 1]))
            kev[f] = b.step("vector", lambda E, f=f, bk=bk: E.tensor_copy(
                ktile[:, f, :], pK[bk][:, 0:128]), waits=kw, inc=V)
        prev_kev = kev[7]
        st_k[i] = b.dma("sync", kT_o[:, :, i * 128:(i + 1) * 128], ktile[:],
                        waits=[(V, kev[7])], inc=ST)
        vev = {}
        for cc in range(2):
            def mm_v(E, bi=bi, cc=cc):
                for c in range(8):
                    ins = E.matmul(pV[cc][:], XTt[:, bi, c, :], wvb[:, c, cc * 512:(cc + 1) * 512],
                                   start=(c == 0), stop=(c == 7))
                return ins
            w = [(V, hv), (G, g_wv)]
            if prev_vev is not None:
                w.append((V, prev_vev))
            tm = b.step("tensor", mm_v, waits=w, inc=T)
            vw = [(T, tm)]
            if cc == 0 and i >= 1:
                vw.append((ST, st_v[i - 1]))
            vev[cc] = b.step("vector", lambda E, cc=cc: E.tensor_copy(
                vtile[:, cc * 512:(cc + 1) * 512], pV[cc][:]), waits=vw, inc=V)
        prev_vev = vev[1]
        st_v[i] = b.dma("sync", v_o[i], vtile[:], waits=[(V, vev[1])], inc=ST)
    b.wait("sync", [(ST, ST.n)])
    return b.build() if b._standalone else None


def build_Dmain(b=None):
    if b is None:
        b = Builder("k", paranoid=True)
        b._standalone = True
    else:
        b._standalone = False
    x_in = b.din("x_in", [2048, 1024], F32)
    kT_all = b.din("kT_all", [8192, 2048], BF16)
    v_all = b.din("v_all", [16384, 1024], BF16)
    wq_in = b.din("wq_in", [1024, 1024], F32)
    wo_in = b.din("wo_in", [1024, 1024], F32)
    n1_in = b.din("n1_in", [1024], F32)
    n2_in = b.din("n2_in", [1024], F32)
    rw_in = b.din("rw_in", [1024, 8], F32)
    shg_in = b.din("shg_in", [1024, 2048], F32)
    shu_in = b.din("shu_in", [1024, 2048], F32)
    shd_in = b.din("shd_in", [2048, 1024], F32)
    eg_in = b.din("eg_in", [8, 1024, 2048], F32)
    eu_in = b.din("eu_in", [8, 1024, 2048], F32)
    ed_in = b.din("ed_in", [8, 2048, 1024], F32)
    idt_in = b.din("idt_in", [128, 128], BF16)
    idtf_in = b.din("idtf_in", [128, 128], F32)
    onesf_in = b.din("onesf_in", [1, 128], F32)
    consts_in = b.din("consts_in", [128, 4], F32)
    x_out = b.dout("x_out", [2048, 1024], F32)

    xt_in = x_in.rearrange("(i p) d -> i p d", p=128)
    yt_out = x_out.rearrange("(i p) d -> i p d", p=128)

    acc = b.sbuf("acc", [128, NT, 1024], F32)
    xbf = b.sbuf("xbf", [128, 2, 1024], BF16)
    XTt = b.sbuf("XTt", [128, 2, 8, 128], BF16)
    scr = b.sbuf("scr", [128, 1024], F32)
    stats = b.sbuf("stats", [128, 8], F32)
    idt = b.sbuf("idt", [128, 128], BF16)
    idtf = b.sbuf("idtf", [128, 128], F32)
    onesf = b.sbuf("onesf", [1, 128], F32)
    consts = b.sbuf("consts", [128, 4], F32)
    w1sb = b.sbuf("w1sb", [128, 8], F32)
    w2sb = b.sbuf("w2sb", [128, 8], F32)
    mxs = b.sbuf("mxs", [128, 4], F32)
    cmx = b.sbuf("cmx", [128, 4], F32)
    sums_c = b.sbuf("sums_c", [128, 4], F32)
    rb = b.sbuf("rb", [128, 128], F32)
    rrow = b.sbuf("rrow", [1, 128], F32)
    me = b.sbuf("me", [128, NT, 8], F32)
    rt = b.sbuf("rt", [128, 8], F32)
    rt2 = b.sbuf("rt2", [128, 8], F32)
    lnin = b.sbuf("lnin", [128, 1, 1024], F32)
    yt = b.sbuf("yt", [128, 1, 1024], F32)

    L = b.sem("L"); G = b.sem("G"); T = b.sem("T"); V = b.sem("V"); S = b.sem("S"); ST = b.sem("ST")
    sems = (L, T, V, S, ST)

    sc1 = b.scope()
    QT = b.sbuf("QT", [128, 8, 2048], BF16, scope=sc1)
    YT = b.sbuf("YT", [128, 8, 1024], BF16, scope=sc1)
    KT2 = b.sbuf("KT2", [128, 8192], BF16, scope=sc1)
    V2 = b.sbuf("V2", [128, 64, 128], BF16, scope=sc1)
    P = b.sbuf("P", [128, 2048], BF16, scope=sc1)
    PT = b.sbuf("PT", [128, 2, 1024], BF16, scope=sc1)
    wqb = b.sbuf("wqb", [128, 8, 1024], BF16, scope=sc1)
    wob = b.sbuf("wob", [128, 8, 1024], BF16, scope=sc1)
    pSb = b.psum("pSb", [128, 2048], F32, scope=sc1)
    pX0 = b.psum("pX0", [128, 1024], BF16, scope=sc1)
    pT_ = b.psum("pT", [128, 1024], BF16, scope=sc1)
    pO = b.psum("pO", [128, 512], F32, scope=sc1)
    pW = [pSb[:, 0:512], pSb[:, 512:1024]]
    tp = TP(b, [pX0, pX0], idt, T, V)

    l_idt = b.dma("sync", idt[:], idt_in, inc=L)
    l_idtf = b.dma("sync", idtf[:], idtf_in, inc=L)
    l_onesf = b.dma("sync", onesf[:], onesf_in, inc=L)
    l_consts = b.dma("sync", consts[:], consts_in, inc=L)
    l_w1 = b.dma("sync", w1sb[:], n1_in.rearrange("(c p) -> p c", p=128), inc=L,
                 allow_slow_non_contiguous=True)
    l_w2 = b.dma("sync", w2sb[:], n2_in.rearrange("(c p) -> p c", p=128), inc=L,
                 allow_slow_non_contiguous=True)
    g_wq = b.dma("gpsimd", wqb[:], wq_in.rearrange("(c p) d -> p c d", p=128), inc=G)

    lx = {}
    for i in range(NT):
        lx[i] = b.dma("sync", acc[:, i, :], xt_in[i], inc=L)
    prev_qev = None
    for i in range(NT):
        bi = i % 2
        nw = [(L, lx[i]), (L, l_consts)]
        if i == 0:
            ev_of = {}
        if i >= 2:
            nw.append((V, ev_of[i - 2]))
        nv = emit_rms_norm_tile(b, sems, acc[:, i, :], scr, stats, consts[:, 1:2],
                                xbf[:, bi, :], waits=nw)
        ev_of[i] = tp.block8(lambda j, bi=bi: xbf[:, bi, j * 128:(j + 1) * 128],
                             XTt[:, bi, :, :], waits=[(V, nv), (L, l_idt)])
        hv = b.step("vector", lambda E, bi=bi: E.tensor_tensor(
            XTt[:, bi, :, :], XTt[:, bi, :, :],
            w1sb[:].unsqueeze(2).broadcast_to([128, 8, 128]), ALU.mult),
            waits=[(V, ev_of[i]), (L, l_w1)], inc=V)
        for hg in range(2):
            def mm_q(E, bi=bi, hg=hg):
                for hp in range(4):
                    for c in range(8):
                        ins = E.matmul(pSb[:, hp * 512 + 0:hp * 512 + 128],
                                       wqb[:, c, (hg * 4 + hp) * 128:(hg * 4 + hp + 1) * 128],
                                       XTt[:, bi, c, :], start=(c == 0), stop=(c == 7))
                return ins
            w = [(V, hv), (G, g_wq)]
            if prev_qev is not None:
                w.append((V, prev_qev))
            tq = b.step("tensor", mm_q, waits=w, inc=T)
            prev_qev = b.step("vector", lambda E, i=i, hg=hg: E.tensor_copy(
                QT[:, hg * 4:(hg + 1) * 4, i * 128:(i + 1) * 128],
                pSb[:].rearrange("p (hp o) -> p hp o", hp=4)[:, :, 0:128]),
                waits=[(T, tq)], inc=V)

    g_wo = b.dma("gpsimd", wob[:], wo_in.rearrange("(c p) d -> p c d", p=128), inc=G)
    v_prev_wo = {0: None, 1: None}
    v_prev_yt = None
    s_prev_exp = None
    t_prev_kv = T.n
    v_ps_guard = prev_qev
    for bb_ in range(2):
        for hp in range(8):
            kvw = [(T, t_prev_kv)]
            if s_prev_exp is not None:
                kvw.append((S, s_prev_exp))
            if v_prev_yt is not None:
                kvw.append((V, v_prev_yt))
            for r in range(8):
                lkt = b.dma("sync", KT2[:, r * 1024:(r + 1) * 1024],
                            kT_all[r * 1024 + hp * 128:r * 1024 + (hp + 1) * 128,
                                   bb_ * 1024:(bb_ + 1) * 1024],
                            waits=(kvw if r == 0 else ()), inc=L)
            vv = v_all.rearrange("(r bb kt p) d -> bb p r kt d", r=8, bb=2, kt=8)
            for r in range(8):
                lv2 = b.dma("sync", V2[:, r * 8:(r + 1) * 8, :],
                            vv[bb_][:, r, :, hp * 128:(hp + 1) * 128], inc=L)
            for tt in range(8):
                for h01 in range(2):
                    base = h01 * 64
                    qsl = QT[base:base + 64, hp, tt * 128:(tt + 1) * 128]
                    red = {}
                    for g in range(4):
                        def mm_s(E, g=g, qsl=qsl, base=base):
                            for j in range(4):
                                kc = g * 4 + j
                                ins = E.matmul(pSb[:, j * 512:(j + 1) * 512], qsl,
                                               KT2[base:base + 64, kc * 512:(kc + 1) * 512],
                                               start=True, stop=True)
                            return ins
                        w = [(L, lkt), (L, lv2)]
                        if g > 0:
                            w.append((V, red[g - 1]))
                        else:
                            w.append((V, v_ps_guard))
                            if s_prev_exp is not None:
                                w.append((S, s_prev_exp))
                        tm = b.step("tensor", mm_s, waits=w, inc=T)
                        red[g] = b.step("vector", lambda E, g=g: E.reduce_max(
                            cmx[:, g:g + 1], pSb[:], axis=AX.X), waits=[(T, tm)], inc=V)
                    v_mx = b.step("vector", lambda E: E.reduce_max(mxs[:, 0:1], cmx[:], axis=AX.X),
                                  waits=[(V, red[3])], inc=V)
                    v_ng = b.step("vector", lambda E: E.tensor_scalar_mul(mxs[:, 1:2], mxs[:, 0:1], -SCALE_D),
                                  waits=[(V, v_mx)], inc=V)
                    v_ps_guard = v_ng
                    s_exp, ptev, pv_of = {}, {}, {}
                    tr_of = {}
                    for g in range(4):
                        def mm_s2(E, g=g, qsl=qsl, base=base):
                            for j in range(4):
                                kc = g * 4 + j
                                ins = E.matmul(pSb[:, j * 512:(j + 1) * 512], qsl,
                                               KT2[base:base + 64, kc * 512:(kc + 1) * 512],
                                               start=True, stop=True)
                            return ins
                        w = [(V, v_ng)]
                        if g > 0:
                            w.append((S, s_exp[g - 1]))
                        tm = b.step("tensor", mm_s2, waits=w, inc=T)
                        we = [(T, tm)]
                        if g > 0:
                            we.append((T, tr_of[(g - 1, 1)]))
                        s_exp[g] = b.step("scalar", lambda E, g=g: E.activation(
                            out=P[:], in_=pSb[:], func=AF.Exp,
                            bias=mxs[:, 1:2], scale=SCALE_D, accum_out=sums_c[:, g:g + 1]),
                            waits=we, inc=S)
                        for half in range(2):
                            wtr = [(S, s_exp[g])]
                            if (g, half) != (0, 0):
                                prevh = (g, half - 1) if half == 1 else (g - 1, 1)
                                wtr.append((V, ptev[prevh]))
                            for j in range(8):
                                tr_of[(g, half)] = b.step("tensor", lambda E, half=half, j=j: E.transpose(
                                    pT_[:, j * 128:(j + 1) * 128],
                                    P[:, half * 1024 + j * 128:half * 1024 + (j + 1) * 128], idt[:]),
                                    waits=(wtr if j == 0 else ()), inc=(T if j == 7 else None))
                            wpt = [(T, tr_of[(g, half)])]
                            if (g, half) >= (1, 0):
                                prevh = (g - 1, half)
                                wpt.append((T, pv_of[prevh]))
                            ptev[(g, half)] = b.step("vector", lambda E, half=half: E.tensor_copy(
                                PT[:, half, :], pT_[:]), waits=wpt, inc=V)
                            def mm_pv(E, g=g, half=half, base=base):
                                for j in range(8):
                                    kt = g * 16 + half * 8 + j
                                    ins = E.matmul(pO[base:base + 64, 0:128],
                                                   V2[:, kt, base:base + 64],
                                                   PT[:, half, j * 128:(j + 1) * 128],
                                                   start=(kt == 0), stop=(kt == 63))
                                return ins
                            w6 = [(V, ptev[(g, half)])]
                            if g == 0 and half == 0 and v_prev_yt is not None:
                                w6.append((V, v_prev_yt))
                            pv_of[(g, half)] = b.step("tensor", mm_pv, waits=w6, inc=T)
                    s_prev_exp = s_exp[3]
                    v_sm = b.step("vector", lambda E: E.reduce_sum(mxs[:, 2:3], sums_c[:], axis=AX.X),
                                  waits=[(S, s_exp[3])], inc=V)
                    v_rc = b.step("vector", lambda E: E.reciprocal(mxs[:, 3:4], mxs[:, 2:3]),
                                  waits=[(V, v_sm)], inc=V)
                    t_tr = b.step("tensor", lambda E: E.transpose(pO[0:1, 256:384], mxs[:, 3:4], idtf[:]),
                                  waits=[(V, v_rc), (T, pv_of[(3, 1)]), (L, l_idtf)], inc=T)
                    v_rr = b.step("vector", lambda E: E.tensor_copy(rrow[:], pO[0:1, 256:384]),
                                  waits=[(T, t_tr)], inc=V)
                    t_bc = b.step("tensor", lambda E, base=base: E.matmul(
                        pO[base:base + 64, 128:256], onesf[:, 0:64], rrow[:], start=True, stop=True),
                        waits=[(L, l_onesf), (V, v_rr)], inc=T)
                    v_rb = b.step("vector", lambda E, base=base: E.tensor_copy(
                        rb[base:base + 64, :], pO[base:base + 64, 128:256]),
                        waits=[(T, t_bc)], inc=V)
                    v_prev_yt = b.step("vector", lambda E, hp=hp, tt=tt, base=base: E.tensor_tensor(
                        YT[base:base + 64, hp, tt * 128:(tt + 1) * 128],
                        pO[base:base + 64, 0:128], rb[base:base + 64, :], ALU.mult),
                        waits=[(V, v_rb)], inc=V)
            t_prev_kv = T.n
        for tt in range(8):
            i = bb_ * 8 + tt
            for cc in range(2):
                def mm_wo(E, tt=tt, cc=cc):
                    for hpx in range(8):
                        ins = E.matmul(pW[cc], YT[:, hpx, tt * 128:(tt + 1) * 128],
                                       wob[:, hpx, cc * 512:(cc + 1) * 512],
                                       start=(hpx == 0), stop=(hpx == 7))
                    return ins
                w7 = [(G, g_wo), (V, v_prev_yt), (S, s_prev_exp)]
                if v_prev_wo[cc] is not None:
                    w7.append((V, v_prev_wo[cc]))
                tw = b.step("tensor", mm_wo, waits=w7, inc=T)
                v_prev_wo[cc] = b.step("vector", lambda E, i=i, cc=cc: E.scalar_tensor_tensor(
                    out=acc[:, i, cc * 512:(cc + 1) * 512], in0=pW[cc], scalar=1.0,
                    in1=acc[:, i, cc * 512:(cc + 1) * 512], op0=ALU.mult, op1=ALU.add),
                    waits=[(T, tw)], inc=V)
        v_ps_guard = V.n
    v_attn_done = V.n
    sc1.close()

    sc2 = b.scope()
    h2T = b.sbuf("h2T", [128, 8, 2048], BF16, scope=sc2)
    gu = b.sbuf("gu", [128, 2, 2, 8, 128], BF16, scope=sc2)
    dn = b.sbuf("dn", [128, 16, 1024], BF16, scope=sc2)
    aT = b.sbuf("aT", [128, 16, 512], BF16, scope=sc2)
    gsb = b.sbuf("gsb", [128, 512], BF16, scope=sc2)
    rwb = b.sbuf("rwb", [128, 8, 8], BF16, scope=sc2)
    pX1 = b.psum("pX1", [128, 1024], BF16, scope=sc2)
    pG = [b.psum("pG0", [128, 512], F32, scope=sc2), b.psum("pG1", [128, 512], F32, scope=sc2)]
    pU = [b.psum("pU0", [128, 512], F32, scope=sc2), b.psum("pU1", [128, 512], F32, scope=sc2)]
    pD = [b.psum("pD0", [128, 512], F32, scope=sc2), b.psum("pD1", [128, 512], F32, scope=sc2)]
    pR = b.psum("pR", [128, 512], F32, scope=sc2)
    tp2 = TP(b, [pX1, pX1], idt, T, V)

    t_attn_done = T.n
    b.wait("gpsimd", [(T, t_attn_done), (V, v_attn_done), (S, s_prev_exp)])
    g_rw = b.dma("gpsimd", rwb[:], rw_in.rearrange("(c p) e -> p c e", p=128), inc=G)

    prev_lg = None
    for i in range(NT):
        bi = i % 2
        nw = [(V, v_attn_done), (L, l_consts)]
        if i == 0:
            ev2_of = {}
        if i >= 2:
            nw.append((V, ev2_of[i - 2]))
        nv = emit_rms_norm_tile(b, sems, acc[:, i, :], scr, stats, consts[:, 1:2],
                                xbf[:, bi, :], waits=nw)
        ev2_of[i] = tp2.block8(lambda j, bi=bi: xbf[:, bi, j * 128:(j + 1) * 128],
                               XTt[:, bi, :, :], waits=[(V, nv)])
        hv = b.step("vector", lambda E, bi=bi: E.tensor_tensor(
            XTt[:, bi, :, :], XTt[:, bi, :, :],
            w2sb[:].unsqueeze(2).broadcast_to([128, 8, 128]), ALU.mult),
            waits=[(V, ev2_of[i]), (L, l_w2)], inc=V)
        hev = b.step("vector", lambda E, bi=bi, i=i: E.tensor_copy(
            h2T[:, :, i * 128:(i + 1) * 128], XTt[:, bi, :, :]),
            waits=[(V, hv)], inc=V)
        def mm_r(E, i=i):
            for c in range(8):
                ins = E.matmul(pR[:, 0:8], h2T[:, c, i * 128:(i + 1) * 128], rwb[:, c, :],
                               start=(c == 0), stop=(c == 7))
            return ins
        w = [(V, hev), (G, g_rw)]
        if prev_lg is not None:
            w.append((V, prev_lg))
        tr_ = b.step("tensor", mm_r, waits=w, inc=T)
        v_lg = b.step("vector", lambda E: E.tensor_copy(rt[:], pR[:, 0:8]), waits=[(T, tr_)], inc=V)
        prev_lg = v_lg
        v_m = b.step("vector", lambda E: E.reduce_max(mxs[:, 0:1], rt[:], axis=AX.X, negate=True),
                     waits=[(V, v_lg)], inc=V)
        s_e = b.step("scalar", lambda E: E.activation(out=rt[:], in_=rt[:], func=AF.Exp,
                                                      bias=mxs[:, 0:1], scale=1.0,
                                                      accum_out=mxs[:, 1:2]),
                     waits=[(V, v_m)], inc=S)
        v_rc = b.step("vector", lambda E: E.reciprocal(mxs[:, 2:3], mxs[:, 1:2]), waits=[(S, s_e)], inc=V)
        v_p = b.step("vector", lambda E: E.tensor_scalar(out=rt[:], in0=rt[:], scalar1=mxs[:, 2:3],
                                                         scalar2=None, op0=ALU.mult),
                     waits=[(V, v_rc)], inc=V)
        v_m1 = b.step("vector", lambda E: E.reduce_max(mxs[:, 0:1], rt[:], axis=AX.X),
                      waits=[(V, v_p)], inc=V)
        v_o1 = b.step("vector", lambda E: E.tensor_scalar(out=rt2[:], in0=rt[:], scalar1=mxs[:, 0:1],
                                                          scalar2=None, op0=ALU.is_ge),
                      waits=[(V, v_m1)], inc=V)
        v_t8 = b.step("vector", lambda E: E.scalar_tensor_tensor(out=scr[:, 0:8], in0=rt2[:], scalar=-2.0,
                                                                 in1=rt[:], op0=ALU.mult, op1=ALU.add),
                      waits=[(V, v_o1)], inc=V)
        v_m2 = b.step("vector", lambda E: E.reduce_max(mxs[:, 1:2], scr[:, 0:8], axis=AX.X),
                      waits=[(V, v_t8)], inc=V)
        v_o2 = b.step("vector", lambda E: E.tensor_scalar(out=scr[:, 0:8], in0=scr[:, 0:8],
                                                          scalar1=mxs[:, 1:2], scalar2=None, op0=ALU.is_ge),
                      waits=[(V, v_m2)], inc=V)
        v_dn = b.step("vector", lambda E: E.tensor_add(mxs[:, 2:3], mxs[:, 0:1], mxs[:, 1:2]),
                      waits=[(V, v_o2)], inc=V)
        v_rd = b.step("vector", lambda E: E.reciprocal(mxs[:, 3:4], mxs[:, 2:3]), waits=[(V, v_dn)], inc=V)
        v_w1 = b.step("vector", lambda E: E.tensor_mul(mxs[:, 0:1], mxs[:, 0:1], mxs[:, 3:4]),
                      waits=[(V, v_rd)], inc=V)
        v_w2 = b.step("vector", lambda E: E.tensor_mul(mxs[:, 1:2], mxs[:, 1:2], mxs[:, 3:4]),
                      waits=[(V, v_w1)], inc=V)
        v_me1 = b.step("vector", lambda E, i=i: E.tensor_scalar(out=me[:, i, :], in0=rt2[:],
                                                                scalar1=mxs[:, 0:1], scalar2=None,
                                                                op0=ALU.mult),
                       waits=[(V, v_w2)], inc=V)
        b.step("vector", lambda E, i=i: E.scalar_tensor_tensor(out=me[:, i, :], in0=scr[:, 0:8],
                                                               scalar=mxs[:, 1:2], in1=me[:, i, :],
                                                               op0=ALU.mult, op1=ALU.add),
               waits=[(V, v_me1)], inc=V)
    v_router_done = V.n

    t_gu_use = {0: None, 1: None}
    v_gu_witness = {0: None, 1: None}
    t_dn_use = None
    prev_aev = None
    prev_stt = {0: None, 1: None}
    s_gsb_prev = None
    for ei in range(9):
        shared = (ei == 8)
        dnw = []
        if prev_stt[0] is not None:
            dnw = [(V, prev_stt[0]), (V, prev_stt[1])]
        g_dn = b.dma("gpsimd", dn[:],
                     (shd_in if shared else ed_in[ei]).rearrange("(c p) d -> p c d", p=128),
                     waits=dnw, inc=G)
        for tg in range(4):
            aev = {}
            for gc in range(16):
                bi = gc % 2
                wl = [(V, v_gu_witness[bi])] if v_gu_witness[bi] is not None else []
                g_gu = b.dma("gpsimd", gu[:, bi, 0, :, :],
                             (shg_in if shared else eg_in[ei])[:, gc * 128:(gc + 1) * 128]
                             .rearrange("(c p) g -> p c g", p=128), waits=wl, inc=G)
                g_gu2 = b.dma("gpsimd", gu[:, bi, 1, :, :],
                              (shu_in if shared else eu_in[ei])[:, gc * 128:(gc + 1) * 128]
                              .rearrange("(c p) g -> p c g", p=128), inc=G)
                ggu = g_gu2
                def mm_g(E, gc=gc, bi=bi, tg=tg):
                    for c in range(8):
                        ins = E.matmul(pG[bi][:], gu[:, bi, 0, c, :], h2T[:, c, tg * 512:(tg + 1) * 512],
                                       start=(c == 0), stop=(c == 7))
                    return ins
                def mm_u(E, gc=gc, bi=bi, tg=tg):
                    for c in range(8):
                        ins = E.matmul(pU[bi][:], gu[:, bi, 1, c, :], h2T[:, c, tg * 512:(tg + 1) * 512],
                                       start=(c == 0), stop=(c == 7))
                    return ins
                w = [(G, ggu), (V, v_router_done)]
                if gc >= 2:
                    w.append((V, aev[gc - 2]))
                elif prev_aev is not None:
                    w.append((V, prev_aev))
                tg_mm = b.step("tensor", mm_g, waits=w, inc=T)
                tu_mm = b.step("tensor", mm_u, inc=T)
                t_gu_use[bi] = tu_mm
                ws = [(T, tu_mm)]
                if s_gsb_prev is not None:
                    ws.append((V, aev[gc - 1] if gc >= 1 else prev_aev))
                sg = b.step("scalar", lambda E, bi=bi: E.activation(out=gsb[:], in_=pG[bi][:],
                                                                    func=AF.Silu),
                            waits=ws, inc=S)
                s_gsb_prev = sg
                aev[gc] = b.step("vector", lambda E, gc=gc, bi=bi: E.tensor_mul(
                    aT[:, gc, :], pU[bi][:], gsb[:]), waits=[(S, sg)], inc=V)
                v_gu_witness[bi] = aev[gc]
            prev_aev = aev[15]
            for tt in range(4):
                i = tg * 4 + tt
                for cc in range(2):
                    def mm_d(E, tt=tt, cc=cc):
                        for gc in range(16):
                            ins = E.matmul(pD[cc][:], aT[:, gc, tt * 128:(tt + 1) * 128],
                                           dn[:, gc, cc * 512:(cc + 1) * 512],
                                           start=(gc == 0), stop=(gc == 15))
                        return ins
                    w = [(V, prev_aev), (G, g_dn)]
                    if prev_stt[cc] is not None:
                        w.append((V, prev_stt[cc]))
                    td = b.step("tensor", mm_d, waits=w, inc=T)
                    if shared:
                        prev_stt[cc] = b.step("vector", lambda E, i=i, cc=cc: E.scalar_tensor_tensor(
                            out=acc[:, i, cc * 512:(cc + 1) * 512], in0=pD[cc][:], scalar=1.0,
                            in1=acc[:, i, cc * 512:(cc + 1) * 512], op0=ALU.mult, op1=ALU.add),
                            waits=[(T, td)], inc=V)
                    else:
                        prev_stt[cc] = b.step("vector", lambda E, i=i, cc=cc, ei=ei: E.scalar_tensor_tensor(
                            out=acc[:, i, cc * 512:(cc + 1) * 512], in0=pD[cc][:], scalar=me[:, i, ei:ei + 1],
                            in1=acc[:, i, cc * 512:(cc + 1) * 512], op0=ALU.mult, op1=ALU.add),
                            waits=[(T, td)], inc=V)
        t_dn_use = T.n
    v_moe_done = V.n

    st_prev = None
    lx2 = {}
    for i in range(NT):
        lx2[i] = b.dma("sync", xbf[:, 0, :].bitcast(F32) if False else lnin[:, 0, :], xt_in[i],
                       waits=[(V, v_moe_done)] if i == 0 else [(V, vadd[i - 1])], inc=L)
        if i == 0:
            vadd = {}
        vadd[i] = b.step("vector", lambda E, i=i: E.tensor_add(acc[:, i, :], acc[:, i, :], lnin[:, 0, :]),
                         waits=[(L, lx2[i]), (V, v_moe_done)], inc=V)
        st_prev = emit_ln_store(b, acc, i, yt, 0, scr, stats, consts,
                                yt_out[i], sems,
                                extra_waits=[(V, vadd[i]), (L, l_consts)], st_prev=st_prev)
    b.wait("sync", [(ST, ST.n)])
    sc2.close()
    return b.build() if b._standalone else None


def host_inputs_pre(I, x_slice):
    return dict(
        x_in=np.ascontiguousarray(x_slice),
        wk_in=I["L1_ffn_wk"], wv_in=I["L1_ffn_wv"], n1_in=I["L1_ffn_norm1_w"],
        idt_in=np.eye(128, dtype=ml_dtypes.bfloat16),
        consts_in=np.tile(np.array([[1e-5, 1e-6, 0, 0]], np.float32), (128, 1)),
    )


def host_inputs_main(I, x_slice, kT_all, v_all):
    return dict(
        x_in=np.ascontiguousarray(x_slice), kT_all=kT_all, v_all=v_all,
        wq_in=I["L1_ffn_wq"], wo_in=I["L1_ffn_wo"],
        n1_in=I["L1_ffn_norm1_w"], n2_in=I["L1_ffn_norm2_w"],
        rw_in=I["L1_ffn_router_w"],
        shg_in=I["L1_ffn_sh_gate"], shu_in=I["L1_ffn_sh_up"], shd_in=I["L1_ffn_sh_down"],
        eg_in=I["L1_ffn_e_gate"], eu_in=I["L1_ffn_e_up"], ed_in=I["L1_ffn_e_down"],
        idt_in=np.eye(128, dtype=ml_dtypes.bfloat16),
        idtf_in=np.eye(128, dtype=np.float32),
        onesf_in=np.ones((1, 128), np.float32),
        consts_in=np.tile(np.array([[1e-5, 1e-6, 0, 0]], np.float32), (128, 1)),
    )


def assemble_kv(pre_results):
    kT_all = np.concatenate([r["kT_out"] for r in pre_results], axis=1)
    v_all = np.concatenate([r["vslc"] for r in pre_results], axis=0)
    return np.ascontiguousarray(kT_all), np.ascontiguousarray(v_all)




def build_L1():
    b = Builder("fusedL1", paranoid="chain")
    b.keep_external = {"C_x_out", "P_kT_out", "P_vslc"}
    b.alias = {
        "B_x_in": "A_x_out",
        "C_x_in": "B_x_out",
        "P_x_in": "C_x_out",
    }
    for ns, fn in [("A_", build_A), ("B_", build_B), ("C_", build_C), ("P_", build_Dpre)]:
        b.prefix = ns
        b.push_scope()
        fn(b)
        b.pop_scope()
    b.prefix = ""
    return b.build()


def host_inputs_L1(I, core, x0):
    m = {}
    for k, v in host_inputs(I, core, x0).items():
        m["A_" + k] = v
    z = np.zeros((1,), np.float32)
    for k, v in host_inputs_B(I, z).items():
        if k != "x_in":
            m["B_" + k] = v
    for k, v in host_inputs_C(I, z).items():
        if k != "x_in":
            m["C_" + k] = v
    for k, v in host_inputs_pre(I, z).items():
        if k != "x_in":
            m["P_" + k] = v
    return m


_NC_CACHE = {}


def _get(name, fn):
    if name not in _NC_CACHE:
        _NC_CACHE[name] = fn()
    return _NC_CACHE[name]


def kernel(**inputs):
    I = inputs
    x0 = np.asarray(I["L0_attn_x"])
    cores = list(range(8))

    nc1 = _get("L1", build_L1)
    res1 = run_bass_kernel_spmd(nc1, [host_inputs_L1(I, c, x0) for c in cores], core_ids=cores)
    x3s = [res1.results[c]["C_x_out"] for c in cores]
    kT_all = np.ascontiguousarray(np.concatenate(
        [res1.results[c]["P_kT_out"] for c in cores], axis=0))
    v_all = np.ascontiguousarray(np.concatenate(
        [res1.results[c]["P_vslc"] for c in cores], axis=0))

    nc2 = _get("L2", lambda: build_Dmain(None))
    res2 = run_bass_kernel_spmd(nc2, [host_inputs_main(I, s, kT_all, v_all) for s in x3s],
                                core_ids=cores)
    return unshard_rows([res2.results[c]["x_out"] for c in cores])
